# version 06 (final)
# baseline (speedup 1.0000x reference)
import math
import numpy as np
import ml_dtypes

import concourse.bass as bass
import concourse.mybir as mybir
from concourse.bass_utils import run_bass_kernel_spmd
from concourse.bass_core_rust import IndirectOffsetOnAxis

F32 = mybir.dt.float32
BF16 = mybir.dt.bfloat16
I32 = mybir.dt.int32
AF = mybir.ActivationFunctionType
ALU = mybir.AluOpType
AX = mybir.AxisListType.X

S, D, H_HEADS, HD, INTER, NEXP = 8192, 2048, 16, 128, 8192, 8
NCORES = 8
TOK = S // NCORES
NCH = S // 128
OCH = TOK // 128
SHIFT = 12.0
SCALE = 1.0 / math.sqrt(HD)
EPS = 1e-5

ENGINES = ("pe", "act", "dve", "pool", "sync")
P2_HEADS = 2
P2_QTILES = 16
ENG_SEM = {e: "E_" + e for e in ENGINES}

C_EPS, C_INVD, C_SCALE, C_NSHIFT, C_NEG1E30 = 0, 1, 2, 3, 4
NCONST = 8


class Builder:
    def __init__(self):
        self.ops = {e: [] for e in ENGINES}
        self.sem_names = [ENG_SEM[e] for e in ENGINES]
        self.eng_count = {e: 0 for e in ENGINES}
        self.dma_count = {}
        self.bufs = {}
        self.waited = {e: {} for e in ENGINES}
        self.sems = {}
        self.nofence_sems = set()

    def dma_sem(self, name):
        if name not in self.dma_count:
            self.dma_count[name] = 0
            self.sem_names.append(name)
        return name

    def _dep_map(self, reads, writes):
        deps = {}
        for b in reads:
            e = self.bufs.get(b)
            if e and e["w"]:
                s, v = e["w"]
                deps[s] = max(deps.get(s, 0), v)
        for b in writes:
            e = self.bufs.get(b)
            if e:
                if e["w"]:
                    s, v = e["w"]
                    deps[s] = max(deps.get(s, 0), v)
                for s, v in e["r"]:
                    deps[s] = max(deps.get(s, 0), v)
        return deps

    def _emit_waits(self, eng, deps, own_sem):
        for s, v in sorted(deps.items()):
            if s == own_sem and eng == "pe":
                continue
            self.wait(eng, s, v)

    def wait(self, eng, sem, val):
        if val <= 0 or self.waited[eng].get(sem, 0) >= val:
            return
        self.waited[eng][sem] = val
        self.ops[eng].append(("w", sem, val))

    def op(self, eng, fn, reads=(), writes=()):
        own = ENG_SEM[eng]
        self._emit_waits(eng, self._dep_map(reads, writes), own)
        self.eng_count[eng] += 1
        val = self.eng_count[eng]
        self.ops[eng].append(("i", fn, own))
        for b in reads:
            self.bufs.setdefault(b, {"w": None, "r": []})["r"].append((own, val))
        for b in writes:
            self.bufs[b] = {"w": (own, val), "r": []}
        return (own, val)

    def dma(self, eng, sem, fn, reads=(), writes=()):
        sem = self.dma_sem(sem)
        self._emit_waits(eng, self._dep_map(reads, writes), None)
        self.dma_count[sem] += 1
        val = 16 * self.dma_count[sem]
        self.ops[eng].append(("d", fn, sem))
        for b in reads:
            self.bufs.setdefault(b, {"w": None, "r": []})["r"].append((sem, val))
        for b in writes:
            self.bufs[b] = {"w": (sem, val), "r": []}
        return (sem, val)

    def dma_group(self, eng, sem, fns, reads=(), writes=()):
        sem = self.dma_sem(sem)
        self._emit_waits(eng, self._dep_map(reads, writes), None)
        for fn in fns:
            self.dma_count[sem] += 1
            self.ops[eng].append(("d", fn, sem))
        val = 16 * self.dma_count[sem]
        for b in reads:
            self.bufs.setdefault(b, {"w": None, "r": []})["r"].append((sem, val))
        for b in writes:
            self.bufs[b] = {"w": (sem, val), "r": []}
        return (sem, val)

    def collective(self, fn, sem, reads=(), writes=()):
        sem = self.dma_sem(sem)
        self.nofence_sems.add(sem)
        self._emit_waits("pool", self._dep_map(reads, writes), None)
        self.dma_count[sem] += 1
        val = self.dma_count[sem]
        self.ops["pool"].append(("c", fn, sem))
        for b in writes:
            self.bufs[b] = {"w": (sem, val), "r": []}
        return (sem, val)

    def fence(self):
        snap_eng = dict(self.eng_count)
        snap_dma = {s: 16 * c for s, c in self.dma_count.items()
                    if s not in self.nofence_sems}
        for e in ENGINES:
            for oe in ENGINES:
                if oe != e and snap_eng[oe] > 0:
                    self.wait(e, ENG_SEM[oe], snap_eng[oe])
            for s, v in snap_dma.items():
                if v > 0:
                    self.wait(e, s, v)

    def replay(self, block):
        sems = self.sems

        def mk(eng_name):
            def body(E):
                for item in self.ops[eng_name]:
                    kind = item[0]
                    if kind == "w":
                        E.wait_ge(sems[item[1]], item[2])
                    elif kind == "i":
                        item[1](E).then_inc(sems[item[2]], 1)
                    elif kind == "d":
                        item[1](E).then_inc(sems[item[2]], 16)
                    else:
                        item[1](E).then_inc(sems[item[2]], 1)
            return body

        block.tensor(mk("pe"))
        block.scalar(mk("act"))
        block.vector(mk("dve"))
        block.gpsimd(mk("pool"))
        block.sync(mk("sync"))


def host_inputs(core, x, Wq, Wk, Wv, Wo, gate_w, up_w, down_w, expert_w,
                ln_gamma, ln_beta, shared):
    c = core
    m = {}
    m["xf"] = shared["xf"]
    m["xo"] = np.ascontiguousarray(shared["xf"][TOK * c:TOK * (c + 1)])
    hc = slice(256 * c, 256 * (c + 1))
    wqk = np.concatenate([Wq[:, hc], Wk[:, hc]], axis=1)
    m["wqk_t"] = np.ascontiguousarray(wqk.reshape(16, 128, 512).transpose(1, 0, 2))
    m["wv_t"] = np.ascontiguousarray(
        Wv[:, hc].reshape(16, 128, 256).transpose(1, 0, 2))
    m.update(shared["static"])
    idx = np.zeros((128, 1), np.int32)
    for p in range(128):
        r, i = p // 16, p % 16
        idx[p, 0] = r * S + TOK * c + 64 * i
    m["ctx_idx"] = idx
    return m


def host_shared(x, Wq, Wk, Wv, Wo, gate_w, up_w, down_w, expert_w,
                ln_gamma, ln_beta):
    xf = np.ascontiguousarray(np.asarray(x).reshape(S, D).astype(np.float32))
    st = {}
    st["wo_t"] = np.ascontiguousarray(
        np.asarray(Wo).reshape(16, 128, 2048).transpose(1, 0, 2))
    st["gate_t"] = np.ascontiguousarray(
        np.asarray(gate_w).reshape(16, 128, 8).transpose(1, 0, 2))
    st["up_t"] = np.ascontiguousarray(
        np.asarray(up_w).reshape(16, 128, 32, 2, 128).transpose(2, 1, 0, 3, 4))
    st["down_t"] = np.ascontiguousarray(
        np.asarray(down_w).reshape(64, 128, 8, 2, 128).transpose(2, 1, 0, 3, 4))
    st["exp_t"] = np.ascontiguousarray(
        np.asarray(expert_w).reshape(16, 128, 8, 2, 128).transpose(2, 1, 0, 3, 4))
    st["lng"] = np.ascontiguousarray(np.asarray(ln_gamma).reshape(1, D).astype(np.float32))
    st["lnb"] = np.ascontiguousarray(np.asarray(ln_beta).reshape(1, D).astype(np.float32))
    freqs = 1.0 / 10000.0 ** (np.arange(0, HD, 2, dtype=np.float32) / HD)
    ang = np.outer(np.arange(S, dtype=np.float32), freqs)
    st["cos"] = np.ascontiguousarray(np.tile(np.cos(ang), (1, 2)).astype(np.float32))
    st["sin"] = np.ascontiguousarray(np.tile(np.sin(ang), (1, 2)).astype(np.float32))
    mk = np.zeros((4, 128, 512), np.float32)
    for b in range(4):
        mk[b] = (128 * b + np.arange(128)[:, None]) <= np.arange(512)[None, :]
    st["masks"] = np.ascontiguousarray(
        mk.reshape(512, 512).astype(ml_dtypes.bfloat16))
    st["ident"] = np.eye(128, dtype=ml_dtypes.bfloat16)
    st["ident_f"] = np.eye(128, dtype=np.float32)
    cons = np.zeros((128, NCONST), np.float32)
    cons[:, C_EPS] = EPS
    cons[:, C_INVD] = 1.0 / D
    cons[:, C_SCALE] = SCALE
    cons[:, C_NSHIFT] = -SHIFT
    cons[:, C_NEG1E30] = -1e30
    st["consts"] = cons
    st["ones_bf"] = np.ones((128, 1), ml_dtypes.bfloat16)
    st["ones_f"] = np.ones((1, 128), np.float32)
    return {"xf": xf, "static": st}


PHASES = ["p1", "p2", "gather", "p3", "p4", "p5", "p6", "p7"]


def build_program(debug=(), upto="p7"):
    _lim = PHASES.index(upto)
    def phase_on(name):
        return PHASES.index(name) <= _lim
    nc = bass.Bass("TRN2", target_bir_lowering=False, debug=False)
    dbg = set(debug)
    B = Builder()

    ap = {}
    def din(name, shape, dtype=F32):
        ap[name] = nc.dram_tensor(name, list(shape), dtype, kind="ExternalInput").ap()
    din("xf", (S, D)); din("xo", (TOK, D))
    din("wqk_t", (128, 16, 512)); din("wv_t", (128, 16, 256))
    din("wo_t", (128, 16, 2048)); din("gate_t", (128, 16, 8))
    din("up_t", (32, 128, 16, 2, 128))
    din("down_t", (8, 128, 64, 2, 128))
    din("exp_t", (8, 128, 16, 2, 128))
    din("lng", (1, D)); din("lnb", (1, D))
    din("cos", (S, 128)); din("sin", (S, 128))
    din("masks", (512, 512), BF16); din("ident", (128, 128), BF16)
    din("ident_f", (128, 128), F32)
    din("consts", (128, NCONST)); din("ones_bf", (128, 1), BF16)
    din("ones_f", (1, 128)); din("ctx_idx", (128, 1), I32)

    out = nc.dram_tensor("out", [TOK, D], F32, kind="ExternalOutput").ap()
    dbg_out = {}
    def ddbg(name, shape, dt=F32):
        if name in dbg:
            dbg_out[name] = nc.dram_tensor(name, list(shape), dt,
                                           kind="ExternalOutput").ap()
    ddbg("d_qkvfm", (6, 128, S), BF16)
    ddbg("d_scr", (128, 768))
    ddbg("d_rope", (128, 768), BF16)
    ddbg("d_cs", (128, 256))
    ddbg("d_ctxtm", (2, 128, S), BF16)
    ddbg("d_vtm", (128, S), BF16)
    ddbg("d_pb", (128, 3 * 1536), BF16)
    ddbg("d_den", (1, 512))
    ddbg("d_cn", (128, 512), BF16)
    ddbg("d_bc", (128, 1024))
    ddbg("d_ctxown", (8, 2, TOK, 128), BF16)
    ddbg("d_x1", (TOK, D))
    ddbg("d_xln2fm", (128, 16 * TOK), BF16)
    ddbg("d_top2", (128, OCH))
    ddbg("d_h", (128, 64 * TOK), BF16)
    ddbg("d_bc2", (128, 1024), BF16)
    ddbg("d_x2s", (128, 16 * TOK), BF16)
    ddbg("d_y", (16, OCH, 128, 128))

    cc_in = [nc.dram_tensor(f"cc_in{h}", [S, 128], BF16).ap() for h in range(2)]
    cc_out = [nc.dram_tensor(f"cc_out{h}", [NCORES * S, 128], BF16,
                             addr_space="Shared").ap()
              for h in range(2)]
    ctx_own = nc.dram_tensor("ctx_own", [8, 2, TOK, 128], BF16).ap()
    x1_dram = nc.dram_tensor("x1_dram", [TOK, D], F32).ap()
    y_dram = nc.dram_tensor("y_dram", [16, OCH, 128, 128], F32).ap()

    SB_TOP = nc.sbuf_top - (nc.sbuf_top % 32)

    def sb(name, shape, dtype, off):
        per = int(np.prod(shape[1:])) * mybir.dt.size(dtype)
        assert off % 32 == 0 and off + per <= SB_TOP, (name, off, per, SB_TOP)
        return nc.alloc_sbuf_tensor_at(name, list(shape), dtype, offset=off)

    PB = (nc.sbuf_base + 31) // 32 * 32
    BASE0 = PB
    consts = sb("consts", [128, NCONST], F32, PB); PB += 32
    ident = sb("ident", [128, 128], BF16, PB); PB += 256
    identf = sb("identf", [128, 128], F32, PB); PB += 512
    onesb = sb("onesb", [128, 1], BF16, PB); PB += 32
    onesf = sb("onesf", [1, 128], F32, PB); PB += 512
    idxsb = sb("idxsb", [128, 1], I32, PB); PB += 32
    small = sb("small", [128, 32], F32, PB); PB += 128
    rsq = sb("rsq", [128, 1024], F32, PB); PB += 4096
    assert PB <= BASE0 + 8192
    A0 = BASE0 + 8192
    A1 = A0 + 96 * 1024
    B0 = A1
    XL2 = SB_TOP - 32 * 1024

    qkvfm = sb("qkvfm", [128, 6 * S], BF16, A0)

    o = B0
    xt = [sb(f"xt{i}", [128, D], F32, o + i * 8192) for i in range(2)]; o += 16384
    sqs = sb("sqs", [128, D], BF16, o); o += 4096
    xln1 = [sb(f"xln1_{i}", [128, D], BF16, o + i * 4096) for i in range(2)]; o += 8192
    xfm = [sb(f"xfm{i}", [128, D], BF16, o + i * 4096) for i in range(2)]; o += 8192
    qkvscr = [sb(f"qkvscr{i}", [128, 768], F32, o + i * 3072) for i in range(2)]
    o += 6144
    qkvbf = [sb(f"qkvbf{i}", [128, 768], BF16, o + i * 1536) for i in range(2)]
    o += 3072
    cost = [sb(f"cost{i}", [128, 128], F32, o + i * 512) for i in range(2)]; o += 1024
    sint = [sb(f"sint{i}", [128, 128], F32, o + i * 512) for i in range(2)]; o += 1024
    wqk_sb = sb("wqk_sb", [128, 16 * 512], BF16, o); o += 16384
    wv_sb = sb("wv_sb", [128, 16 * 256], BF16, o); o += 8192
    assert o <= XL2

    o = B0
    vtm = sb("vtm", [128, S], BF16, o); o += 16384
    ctxtm = [sb(f"ctxtm{h}", [128, S], BF16, o + h * 16384) for h in range(2)]; o += 32768
    masks_sb = sb("masks_sb", [128, 2048], BF16, o); o += 4096
    pb = [sb(f"pb{i}", [128, 1536], BF16, o + i * 3072) for i in range(3)]; o += 9216
    padd = [sb(f"padd{i}", [128, 512], BF16, o + i * 1024) for i in range(2)]; o += 2048
    cnsb = sb("cnsb", [128, 512], BF16, o); o += 1024
    bcsb = sb("bcsb", [128, 1024], F32, o); o += 4096
    srow = sb("srow", [1, 512], F32, o); o += 2048
    assert o <= XL2

    xln2fm = sb("xln2fm", [128, 16 * TOK], BF16, XL2)
    wo_sb = sb("wo_sb", [128, 16 * D], BF16, A0)
    gth = [sb(f"gth{h}", [128, 8192], BF16, A0 + 65536 + h * 16384) for h in range(2)]
    o = B0
    ctxT = [sb(f"ctxT{i}", [128, D], BF16, o + i * 4096) for i in range(2)]; o += 8192
    x1b = [sb(f"x1b{i}", [128, D], F32, o + i * 8192) for i in range(2)]; o += 16384
    xob = [sb(f"xob{i}", [128, D], F32, o + i * 8192) for i in range(2)]; o += 16384
    xl2t = sb("xl2t", [128, D], BF16, o); o += 4096
    sq3 = [sb(f"sq3_{i}", [128, D], F32, o + i * 8192) for i in range(2)]; o += 16384
    assert o <= XL2
    o4 = A0 + 131072 + 16384 + 8192 + 8192 + 4096
    lg_sb = sb("lg_sb", [128, 32], F32, o4); o4 += 128
    t2tmp = sb("t2tmp", [8, 128], F32, o4); o4 += 512
    top2c = sb("top2c", [128, OCH], F32, o4); o4 += 32
    gsb = sb("gsb", [128, 16 * NEXP], BF16, o4); o4 += 256
    bc2 = sb("bc2", [128, 1024], BF16, o4); o4 += 2048
    assert o4 <= XL2, o4
    sr2 = sb("sr2", [1, 1024], F32, A0 + 131072 + 16384 + 8192)

    H_sb = sb("H_sb", [128, 64 * TOK], BF16, A0)
    HEND = A0 + 128 * 1024
    o = HEND
    wup = [sb(f"wup{i}", [128, 4096], BF16, o + i * 8192) for i in range(2)]; o += 16384
    wdn = [sb(f"wdn{i}", [128, 2048], BF16, o + i * 4096) for i in range(2)]; o += 8192
    rsc = sb("rsc", [128, 1024], F32, o)
    ysc = [sb(f"ysc{i}", [128, 1024], F32, o + i * 4096) for i in range(2)]
    o += 8192
    yst = sb("yst", [128, 1024], F32, o); o += 4096
    assert o <= XL2

    o = A0
    gamb = sb("gamb", [128, D], F32, o); o += 8192
    betb = sb("betb", [128, D], F32, o); o += 8192
    grow = sb("grow", [1, D], F32, o); o += 8192
    browt = sb("browt", [1, D], F32, o); o += 8192
    yb = [sb(f"yb{i}", [128, D], F32, o + i * 8192) for i in range(2)]; o += 16384
    x1r = [sb(f"x1r{i}", [128, D], F32, o + i * 8192) for i in range(2)]; o += 16384
    ob = [sb(f"ob{i}", [128, D], F32, o + i * 8192) for i in range(2)]; o += 16384
    yb.append(sb("yb2", [128, D], F32, B0))
    x1r.append(sb("x1r2", [128, D], F32, B0 + 8192))
    ob.append(sb("ob2", [128, D], F32, B0 + 16384))
    assert o <= A0 + 131072

    psA = nc.alloc_psum_tensor("psA", [128, 1536], F32)
    psB = nc.alloc_psum_tensor("psB", [128, 1536], F32)
    psC = nc.alloc_psum_tensor("psC", [128, 512], F32)
    psD = nc.alloc_psum_tensor("psD", [128, 512], F32)
    psC_bf = psC.ap().bitcast(BF16)
    psD_bf = psD.ap().bitcast(BF16)

    cons = consts.ap()
    c_eps = cons[:, C_EPS:C_EPS + 1]
    c_invd = cons[:, C_INVD:C_INVD + 1]
    c_scale = cons[:, C_SCALE:C_SCALE + 1]
    c_nshift = cons[:, C_NSHIFT:C_NSHIFT + 1]
    c_neg = cons[:, C_NEG1E30:C_NEG1E30 + 1]

    sm = small.ap()

    class Stats:
        def __init__(self, par):
            base = 16 * par
            (self.sum, self.sq, self.m, self.var, self.rstd, self.nmr,
             self.t0, self.t1) = [sm[:, base + i:base + i + 1] for i in range(8)]
            self.k = {n: f"s{par}_{n}" for n in
                      ("sum", "sq", "m", "var", "rstd", "nmr", "t0", "t1")}

    STATS = [Stats(0), Stats(1)]

    def ln_finalize(S):
        k = S.k
        B.op("dve", lambda E: E.tensor_scalar(S.m, S.sum, c_invd, None, ALU.mult),
             reads=[k["sum"], "consts"], writes=[k["m"]])
        B.op("dve", lambda E: E.tensor_scalar(S.t0, S.sq, c_invd, None, ALU.mult),
             reads=[k["sq"], "consts"], writes=[k["t0"]])
        B.op("dve", lambda E: E.tensor_tensor(S.t1, S.m, S.m, ALU.mult),
             reads=[k["m"]], writes=[k["t1"]])
        B.op("dve", lambda E: E.tensor_tensor(S.var, S.t0, S.t1, ALU.subtract),
             reads=[k["t0"], k["t1"]], writes=[k["var"]])
        B.op("act", lambda E: E.activation(S.t0, S.var, AF.Sqrt, bias=c_eps),
             reads=[k["var"], "consts"], writes=[k["t0"]])
        B.op("dve", lambda E: E.reciprocal(S.rstd, S.t0),
             reads=[k["t0"]], writes=[k["rstd"]])
        B.op("dve", lambda E: E.scalar_tensor_tensor(
            S.nmr, S.m, -1.0, S.rstd, ALU.mult, ALU.mult),
            reads=[k["m"], k["rstd"]], writes=[k["nmr"]])

    _pre = [("consts", consts, "consts"), ("ident", ident, "ident"),
            ("ident_f", identf, "identf"), ("ones_bf", onesb, "onesb"),
            ("ones_f", onesf, "onesf"), ("ctx_idx", idxsb, "idxsb")]
    B.dma_group("sync", "q_pre",
                [lambda E, n=n, t=t: E.dma_start(out=t.ap()[:, :], in_=ap[n])
                 for n, t, _ in _pre],
                writes=[k for _, _, k in _pre])
    B.dma_group("pool", "q_w0", [
        lambda E: E.dma_start(
            out=wqk_sb.ap()[:, :].rearrange("p (k f) -> p k f", k=16),
            in_=ap["wqk_t"]),
        lambda E: E.dma_start(
            out=wv_sb.ap()[:, :].rearrange("p (k f) -> p k f", k=16),
            in_=ap["wv_t"]),
    ], writes=["wqk_sb", "wv_sb"])

    def p1_s1(t):
        par = t % 2
        xa = xt[par].ap()
        xk, xlk = f"xt{par}", f"xln1_{par}"
        B.dma("sync", f"q_x{par}", lambda E, t=t, xa=xa: E.dma_start(
            out=xa[:, :], in_=ap["xf"][128 * t:128 * (t + 1), :]), writes=[xk])
        B.dma_group("sync", f"q_cs{par}", [
            lambda E, t=t, p=par: E.dma_start(
                out=cost[p].ap()[:, :], in_=ap["cos"][128 * t:128 * (t + 1), :]),
            lambda E, t=t, p=par: E.dma_start(
                out=sint[p].ap()[:, :], in_=ap["sin"][128 * t:128 * (t + 1), :]),
        ], writes=[f"cost{par}", f"sint{par}"])
        St = STATS[par]
        B.op("dve", lambda E, xa=xa, St=St: E.tensor_scalar(
            xa[:, :], xa[:, :], 1.0, None, ALU.mult, ALU.add, accum_out=St.sum),
            reads=[xk], writes=[xk, St.k["sum"]])
        B.op("act", lambda E, xa=xa, St=St: E.activation(
            sqs.ap()[:, :], xa[:, :], AF.Square, accum_out=St.sq),
            reads=[xk], writes=[St.k["sq"]])
        ln_finalize(St)
        if par == 0:
            B.op("dve", lambda E, xa=xa, p=par, St=St: E.tensor_scalar(
                xln1[p].ap()[:, :], xa[:, :], St.rstd, St.nmr, ALU.mult, ALU.add),
                reads=[xk, St.k["rstd"], St.k["nmr"]], writes=[xlk])
        else:
            B.op("act", lambda E, xa=xa, p=par, St=St: E.activation(
                xln1[p].ap()[:, :], xa[:, :], AF.Identity, bias=St.nmr, scale=St.rstd),
                reads=[xk, St.k["rstd"], St.k["nmr"]], writes=[xlk])

    def p1_s2t(t):
        par = t % 2
        xlk, xfk = f"xln1_{par}", f"xfm{par}"
        pst = psD_bf if par == 0 else psC_bf
        pstk = "ps_d" if par == 0 else "ps_c"
        for half in range(2):
            for i in range(8):
                k = 8 * half + i
                B.op("pe", lambda E, p=par, k=k, i=i, pst=pst: E.transpose(
                    pst[:, 128 * i:128 * (i + 1)],
                    xln1[p].ap()[:, 128 * k:128 * (k + 1)], ident.ap()[:, :]),
                    reads=[xlk, "ident", pstk], writes=[pstk])
            eng = "dve" if (t + half) % 2 == 0 else "act"
            if eng == "dve":
                B.op("dve", lambda E, p=par, h=half, pst=pst: E.tensor_copy(
                    xfm[p].ap()[:, 1024 * h:1024 * (h + 1)], pst[:, :]),
                    reads=[pstk], writes=[xfk, pstk])
            else:
                B.op("act", lambda E, p=par, h=half, pst=pst: E.activation(
                    xfm[p].ap()[:, 1024 * h:1024 * (h + 1)], pst[:, :], AF.Copy),
                    reads=[pstk], writes=[xfk, pstk])

    def p1_s2m(t):
        par = t % 2
        xfk = f"xfm{par}"
        ps = psA.ap() if par == 0 else psB.ap()
        psk = "ps_A" if par == 0 else "ps_B"
        for k in range(16):
            B.op("pe", lambda E, p=par, k=k, ps=ps: E.matmul(
                ps[:, 0:512], xfm[p].ap()[:, 128 * k:128 * (k + 1)],
                wqk_sb.ap()[:, 512 * k:512 * (k + 1)],
                start=(k == 0), stop=(k == 15)),
                reads=[xfk, "wqk_sb", psk], writes=[psk])
        for k in range(16):
            B.op("pe", lambda E, p=par, k=k, ps=ps: E.matmul(
                ps[:, 512:768], xfm[p].ap()[:, 128 * k:128 * (k + 1)],
                wv_sb.ap()[:, 256 * k:256 * (k + 1)],
                start=(k == 0), stop=(k == 15)),
                reads=[xfk, "wv_sb", psk], writes=[psk])
        B.op("act", lambda E, ps=ps, par=par: E.activation(
            qkvscr[par].ap()[:, :], ps[:, 0:768], AF.Copy),
            reads=[psk], writes=[f"qkvscr{par}", psk])
        qs, qb = qkvscr[par].ap(), qkvbf[par].ap()
        cs, sn = cost[par].ap(), sint[par].ap()
        ck, sk = f"cost{par}", f"sint{par}"
        scrk, qbk = f"qkvscr{par}", f"qkvbf{par}"

        def rope(engine, base, t0, t1, k0, k1, cs=cs, sn=sn, ck=ck, sk=sk):
            xe = qs[:, base + 0:base + 256:2]
            xo = qs[:, base + 1:base + 256:2]
            oe = qb[:, base + 0:base + 256:2]
            oo = qb[:, base + 1:base + 256:2]
            B.op(engine, lambda E, t0=t0, xe=xe, cs=cs: E.tensor_tensor(t0, xe, cs[:, :], ALU.mult),
                 reads=[scrk, ck], writes=[k0])
            B.op(engine, lambda E, t1=t1, xo=xo, sn=sn: E.tensor_tensor(t1, xo, sn[:, :], ALU.mult),
                 reads=[scrk, sk], writes=[k1])
            B.op(engine, lambda E, oe=oe, t0=t0, t1=t1: E.tensor_tensor(oe, t0, t1, ALU.subtract),
                 reads=[k0, k1], writes=[qbk])
            B.op(engine, lambda E, t0=t0, xe=xe, sn=sn: E.tensor_tensor(t0, xe, sn[:, :], ALU.mult),
                 reads=[scrk, sk], writes=[k0])
            B.op(engine, lambda E, t1=t1, xo=xo, cs=cs: E.tensor_tensor(t1, xo, cs[:, :], ALU.mult),
                 reads=[scrk, ck], writes=[k1])
            B.op(engine, lambda E, oo=oo, t0=t0, t1=t1: E.tensor_tensor(oo, t0, t1, ALU.add),
                 reads=[k0, k1], writes=[qbk])

        rbase = 512 * par
        rope("dve", 0, rsq.ap()[:, rbase:rbase + 128],
             rsq.ap()[:, rbase + 128:rbase + 256], f"rq0_{par}", f"rq1_{par}")
        rope("pool", 256, rsq.ap()[:, rbase + 256:rbase + 384],
             rsq.ap()[:, rbase + 384:rbase + 512], f"rk0_{par}", f"rk1_{par}")
        B.op("pool", lambda E, qb=qb, qs=qs: E.tensor_copy(qb[:, 512:768], qs[:, 512:768]),
             reads=[scrk], writes=[qbk])

    def p1_s2r(t):
        par = t % 2
        qb = qkvbf[par].ap()
        qbk = f"qkvbf{par}"
        pst2 = psC_bf if par == 0 else psD_bf
        pst2k = "ps_c" if par == 0 else "ps_d"
        srcs = [0, 256, 512, 128, 384, 640]
        for j, off in enumerate(srcs):
            B.op("pe", lambda E, off=off, j=j, qb=qb, pst2=pst2: E.transpose(
                pst2[:, 128 * j:128 * (j + 1)], qb[:, off:off + 128],
                ident.ap()[:, :]),
                reads=[qbk, "ident", pst2k], writes=[pst2k])
        dst = qkvfm.ap()[:, :].rearrange("p (s n) -> p s n", s=6)[
            :, :, 128 * t:128 * (t + 1)]
        eng = "act" if par == 0 else "dve"
        if eng == "act":
            B.op("act", lambda E, dst=dst, pst2=pst2: E.activation(
                dst, pst2[:, 0:768].rearrange("p (s n) -> p s n", s=6), AF.Copy),
                reads=[pst2k], writes=[f"qkvfm_{t}", pst2k])
        else:
            B.op("dve", lambda E, dst=dst, pst2=pst2: E.tensor_copy(
                dst, pst2[:, 0:768].rearrange("p (s n) -> p s n", s=6)),
                reads=[pst2k], writes=[f"qkvfm_{t}", pst2k])

    p1_s1(0)
    p1_s1(1)
    p1_s2t(0)
    for t in range(NCH):
        if t + 1 < NCH:
            p1_s2t(t + 1)
        p1_s2m(t)
        if t >= 1:
            p1_s2r(t - 1)
        if t + 2 < NCH:
            p1_s1(t + 2)
    p1_s2r(NCH - 1)
    qkv_all = [f"qkvfm_{t}" for t in range(NCH)]

    if "d_scr" in dbg_out:
        B.dma("act", "q_dbg", lambda E: E.dma_start(
            out=dbg_out["d_scr"], in_=qkvscr.ap()[:, :]), reads=["qkvscr"])
    if "d_rope" in dbg_out:
        B.dma("act", "q_dbg", lambda E: E.dma_start(
            out=dbg_out["d_rope"], in_=qkvbf.ap()[:, :]), reads=["qkvbf"])
    if "d_cs" in dbg_out:
        B.dma("act", "q_dbg", lambda E: E.dma_start(
            out=dbg_out["d_cs"][:, 0:128], in_=cost[1].ap()[:, :]), reads=["cost1"])
        B.dma("act", "q_dbg", lambda E: E.dma_start(
            out=dbg_out["d_cs"][:, 128:256], in_=sint[1].ap()[:, :]), reads=["sint1"])

    if "d_qkvfm" in dbg_out:
        for sl in range(6):
            B.dma("act", "q_dbg", lambda E, sl=sl: E.dma_start(
                out=dbg_out["d_qkvfm"][sl], in_=qkvfm.ap()[:, sl * S:(sl + 1) * S]),
                reads=qkv_all)

    if not phase_on("p2"):
        _finalize_build(nc, B)
        return _emit(nc, B)
    B.fence()
    B.dma("sync", "q_msk", lambda E: E.dma_start(
        out=masks_sb.ap()[:, :].rearrange("p (b q) -> p b q", b=4),
        in_=ap["masks"].rearrange("(b k) q -> k b q", b=4)),
        writes=["masks_sb"])
    for h in range(P2_HEADS):
        qslot, kslot, vslot = 3 * h, 3 * h + 1, 3 * h + 2
        qv = qkvfm.ap()[:, qslot * S:(qslot + 1) * S]
        kv = qkvfm.ap()[:, kslot * S:(kslot + 1) * S]
        vv = qkvfm.ap()[:, vslot * S:(vslot + 1) * S]
        for b in range(8):
            for i in range(8):
                cidx = 8 * b + i
                B.op("pe", lambda E, cidx=cidx, i=i, vv=vv: E.transpose(
                    psD_bf[:, 128 * i:128 * (i + 1)],
                    vv[:, 128 * cidx:128 * (cidx + 1)], ident.ap()[:, :]),
                    reads=qkv_all + ["ident", "ps_d"], writes=["ps_d"])
            B.op("dve", lambda E, b=b: E.tensor_copy(
                vtm.ap()[:, 1024 * b:1024 * (b + 1)], psD_bf[:, :]),
                reads=["ps_d"], writes=["vtm", "ps_d"])
        for qi in range(P2_QTILES):
            nk = 4 * (qi + 1)
            qrhs = qv[:, 512 * qi:512 * (qi + 1)]
            groups = []
            kc0 = 0
            while kc0 < nk:
                groups.append((kc0, min(3, nk - kc0)))
                kc0 += 3
            ng = len(groups)

            def emit_S(gi):
                kc0, gw = groups[gi]
                par = gi % 2
                ps = psA.ap() if par == 0 else psB.ap()
                psk = "ps_A" if par == 0 else "ps_B"
                for j in range(gw):
                    kc = kc0 + j
                    B.op("pe", lambda E, kc=kc, j=j, ps=ps, kv=kv, qrhs=qrhs:
                         E.matmul(
                        ps[:, 512 * j:512 * (j + 1)],
                        kv[:, 128 * kc:128 * (kc + 1)], qrhs,
                        start=True, stop=True),
                        reads=qkv_all + [psk], writes=[psk])

            def emit_exp_mask(gi):
                kc0, gw = groups[gi]
                par = gi % 2
                ps = psA.ap() if par == 0 else psB.ap()
                psk = "ps_A" if par == 0 else "ps_B"
                pbk = f"pb{gi % 3}"
                pba = pb[gi % 3].ap()
                B.op("act", lambda E, gw=gw, ps=ps, pba=pba: E.activation(
                    pba[:, 0:512 * gw], ps[:, 0:512 * gw], AF.Exp,
                    bias=c_nshift, scale=c_scale),
                    reads=[psk, "consts"], writes=[pbk, psk])
                for j in range(gw):
                    kc = kc0 + j
                    if kc >= nk - 4:
                        bmask = kc - (nk - 4)
                        B.op("dve", lambda E, j=j, bm=bmask, pba=pba: E.tensor_tensor(
                            pba[:, 512 * j:512 * (j + 1)],
                            pba[:, 512 * j:512 * (j + 1)],
                            masks_sb.ap()[:, 512 * bm:512 * (bm + 1)], ALU.mult),
                            reads=[pbk, "masks_sb"], writes=[pbk])

            def emit_den_pv(gi):
                kc0, gw = groups[gi]
                pbk = f"pb{gi % 3}"
                pba = pb[gi % 3].ap()
                pk = f"padd{gi % 2}"
                pa = padd[gi % 2].ap()
                if gw == 1:
                    dsrc, dsk = pba[:, 0:512], pbk
                else:
                    B.op("dve", lambda E, pba=pba, pa=pa: E.tensor_tensor(
                        pa[:, :], pba[:, 0:512], pba[:, 512:1024], ALU.add),
                        reads=[pbk], writes=[pk])
                    if gw == 3:
                        B.op("dve", lambda E, pba=pba, pa=pa: E.tensor_tensor(
                            pa[:, :], pa[:, :], pba[:, 1024:1536], ALU.add),
                            reads=[pk, pbk], writes=[pk])
                    dsrc, dsk = pa[:, :], pk
                B.op("pe", lambda E, dsrc=dsrc, st=(gi == 0), sp=(gi == ng - 1):
                     E.matmul(
                    psD.ap()[0:1, 0:512], onesb.ap()[:, :], dsrc,
                    start=st, stop=sp),
                    reads=[dsk, "onesb", "ps_d"], writes=["ps_d"])
                for j in range(gw):
                    kc = kc0 + j
                    B.op("pe", lambda E, kc=kc, j=j, pba=pba,
                         st=(gi == 0 and j == 0),
                         sp=(gi == ng - 1 and j == gw - 1): E.matmul(
                        psC.ap()[:, 0:512], vtm.ap()[:, 128 * kc:128 * (kc + 1)],
                        pba[:, 512 * j:512 * (j + 1)], start=st, stop=sp),
                        reads=["vtm", pbk, "ps_ctx"], writes=["ps_ctx"])

            emit_S(0)
            for gi in range(ng):
                if gi + 1 < ng:
                    emit_S(gi + 1)
                emit_exp_mask(gi)
                emit_den_pv(gi)
            B.op("dve", lambda E: E.reciprocal(
                srow.ap()[0:1, 0:512], psD.ap()[0:1, 0:512]),
                reads=["ps_d"], writes=["srow", "ps_d"])
            B.op("pe", lambda E: E.matmul(
                psD.ap()[:, 0:512], onesf.ap()[:, :], srow.ap()[0:1, 0:512],
                start=True, stop=True),
                reads=["srow", "onesf", "ps_d"], writes=["ps_d"])
            B.op("act", lambda E: E.activation(
                bcsb.ap()[:, 0:512], psD.ap()[:, 0:512], AF.Copy),
                reads=["ps_d"], writes=["bcsb", "ps_d"])
            B.op("dve", lambda E: E.tensor_tensor(
                cnsb.ap()[:, :], psC.ap()[:, 0:512], bcsb.ap()[:, 0:512], ALU.mult),
                reads=["ps_ctx", "bcsb"], writes=["cnsb", "ps_ctx"])
            for j in range(4):
                B.op("pe", lambda E, j=j: E.transpose(
                    psD_bf[:, 128 * j:128 * (j + 1)],
                    cnsb.ap()[:, 128 * j:128 * (j + 1)], ident.ap()[:, :]),
                    reads=["cnsb", "ident", "ps_d"], writes=["ps_d"])
            B.op("dve", lambda E, h=h, qi=qi: E.tensor_copy(
                ctxtm[h].ap()[:, 512 * qi:512 * (qi + 1)], psD_bf[:, 0:512]),
                reads=["ps_d"], writes=[f"ctxtm{h}_{qi}", "ps_d"])
        ctxkeys = [f"ctxtm{h}_{qi}" for qi in range(P2_QTILES)]
        if P2_HEADS < 2 or P2_QTILES < 16:
            B.dma("act", f"q_cc{h}", lambda E, h=h: E.dma_start(
                out=cc_in[h].rearrange("(c p) d -> p c d", p=128),
                in_=ctxtm[h].ap()[:, :].rearrange("p (c d) -> p c d", d=128)),
                reads=ctxkeys, writes=[f"cc_in{h}"])
            B.dma_group("act", f"q_ag{h}", [
                (lambda E, h=h, r=r: E.dma_start(
                    out=cc_out[h][S * r:S * (r + 1), :], in_=cc_in[h][:, :]))
                for r in range(NCORES)],
                reads=[f"cc_in{h}"], writes=[f"cc_out{h}"])
            continue
        B.dma("act", f"q_cc{h}", lambda E, h=h: E.dma_start(
            out=cc_in[h].rearrange("(c p) d -> p c d", p=128),
            in_=ctxtm[h].ap()[:, :].rearrange("p (c d) -> p c d", d=128)),
            reads=ctxkeys, writes=[f"cc_in{h}"])
        B.collective(lambda E, h=h: E.collective_compute(
            "AllGather", ALU.bypass, replica_groups=[list(range(NCORES))],
            ins=[cc_in[h][:, :].opt()], outs=[cc_out[h][:, :].opt()]),
            f"q_ag{h}", reads=[f"cc_in{h}"], writes=[f"cc_out{h}"])

    if "d_ctxtm" in dbg_out:
        for h in range(2):
            B.dma("act", "q_dbg", lambda E, h=h: E.dma_start(
                out=dbg_out["d_ctxtm"][h], in_=ctxtm[h].ap()[:, :]),
                reads=[f"ctxtm{h}_{qi}" for qi in range(16)])
    if "d_vtm" in dbg_out:
        B.dma("act", "q_dbg", lambda E: E.dma_start(
            out=dbg_out["d_vtm"], in_=vtm.ap()[:, :]), reads=["vtm"])
    if "d_pb" in dbg_out:
        for i in range(3):
            B.dma("act", "q_dbg", lambda E, i=i: E.dma_start(
                out=dbg_out["d_pb"][:, 1536 * i:1536 * (i + 1)],
                in_=pb[i].ap()[:, :]), reads=[f"pb{i}"])
    if "d_den" in dbg_out:
        B.dma("act", "q_dbg", lambda E: E.dma_start(
            out=dbg_out["d_den"], in_=srow.ap()[:, 0:512]), reads=["srow"])
    if "d_cn" in dbg_out:
        B.dma("act", "q_dbg", lambda E: E.dma_start(
            out=dbg_out["d_cn"], in_=cnsb.ap()[:, :]), reads=["cnsb"])
    if "d_bc" in dbg_out:
        B.dma("act", "q_dbg", lambda E: E.dma_start(
            out=dbg_out["d_bc"], in_=bcsb.ap()[:, :]), reads=["bcsb"])

    if not phase_on("gather"):
        _finalize_build(nc, B)
        return _emit(nc, B)
    B.fence()
    B.dma("pool", "q_wo", lambda E: E.dma_start(
        out=wo_sb.ap()[:, :].rearrange("p (k f) -> p k f", k=16), in_=ap["wo_t"]),
        writes=["wo_sb"])
    for h in range(2):
        B.dma("pool", f"q_gth{h}", lambda E, h=h: E.indirect_dma_start(
            gth[h].ap()[:, :], None, cc_out[h],
            IndirectOffsetOnAxis(idxsb.ap()[:, :], 0)),
            reads=[f"cc_out{h}", "idxsb"], writes=[f"gth{h}"])
        B.dma_group("pool", f"q_cto{h}", [
            (lambda E, h=h, r=r: E.dma_start(
                out=ctx_own[r, h].rearrange("(i t) d -> i (t d)", i=16),
                in_=gth[h].ap()[16 * r:16 * (r + 1), :]))
            for r in range(8)], reads=[f"gth{h}"], writes=[f"ctx_own{h}"])
    if "d_ctxown" in dbg_out:
        B.dma("act", "q_dbg", lambda E: E.dma_start(
            out=dbg_out["d_ctxown"].rearrange("r h t d -> (r h) (t d)"),
            in_=ctx_own.rearrange("r h t d -> (r h) (t d)")),
            reads=["ctx_own0", "ctx_own1"])

    if not phase_on("p3"):
        _finalize_build(nc, B)
        return _emit(nc, B)
    for t in range(OCH):
        par = t % 2
        ctk = f"ctxT{par}"
        B.dma_group("sync", f"q_ctxT{par}", [
            (lambda E, r=g // 2, hh=g % 2, t=t, g=g, par=par:
             E.dma_start(out=ctxT[par].ap()[:, 128 * g:128 * (g + 1)],
                         in_=ctx_own[r, hh, 128 * t:128 * (t + 1), :],
                         transpose=True))
            for g in range(16)],
            reads=["ctx_own0", "ctx_own1"], writes=[ctk])
        B.dma("sync", f"q_xo{par}", lambda E, t=t, par=par: E.dma_start(
            out=xob[par].ap()[:, :], in_=ap["xo"][128 * t:128 * (t + 1), :]),
            writes=[f"xob{par}"])
        outs = [psA.ap()[:, 0:512], psA.ap()[:, 512:1024],
                psA.ap()[:, 1024:1536], psB.ap()[:, 0:512]]
        pkeys = ["ps_A", "ps_A", "ps_A", "ps_B"]
        for n in range(4):
            for k in range(16):
                B.op("pe", lambda E, n=n, k=k, par=par: E.matmul(
                    outs[n], ctxT[par].ap()[:, 128 * k:128 * (k + 1)],
                    wo_sb.ap()[:, 2048 * k + 512 * n:2048 * k + 512 * (n + 1)],
                    start=(k == 0), stop=(k == 15)),
                    reads=[ctk, "wo_sb", pkeys[n]], writes=[pkeys[n]])
        St = STATS[par]
        sqa = sq3[par].ap()
        sqk = f"sq3_{par}"
        B.op("dve", lambda E, sqa=sqa, St=St: E.tensor_scalar(
            sqa[:, 0:1536], psA.ap()[:, :], 1.0, None, ALU.mult, ALU.add,
            accum_out=St.sum),
            reads=["ps_A"], writes=[sqk, St.k["sum"]])
        B.op("dve", lambda E, sqa=sqa, St=St: E.tensor_scalar(
            sqa[:, 1536:2048], psB.ap()[:, 0:512], 1.0, None, ALU.mult, ALU.add,
            accum_out=St.t0),
            reads=["ps_B"], writes=[sqk, St.k["t0"]])
        B.op("act", lambda E, sqa=sqa, St=St: E.activation(
            sqa[:, 0:1536], psA.ap()[:, :], AF.Square, accum_out=St.sq),
            reads=["ps_A", sqk], writes=[sqk, St.k["sq"]])
        B.op("act", lambda E, sqa=sqa, St=St: E.activation(
            sqa[:, 1536:2048], psB.ap()[:, 0:512], AF.Square, accum_out=St.t1),
            reads=["ps_B", sqk], writes=[sqk, St.k["t1"]])
        B.op("dve", lambda E, St=St: E.tensor_tensor(St.sum, St.sum, St.t0, ALU.add),
             reads=[St.k["sum"], St.k["t0"]], writes=[St.k["sum"]])
        B.op("dve", lambda E, St=St: E.tensor_tensor(St.sq, St.sq, St.t1, ALU.add),
             reads=[St.k["sq"], St.k["t1"]], writes=[St.k["sq"]])
        ln_finalize(St)
        B.op("dve", lambda E, par=par, St=St: E.tensor_scalar(
            x1b[par].ap()[:, 0:1536], psA.ap()[:, :], St.rstd, St.nmr,
            ALU.mult, ALU.add),
            reads=["ps_A", St.k["rstd"], St.k["nmr"]], writes=[f"x1b{par}", "ps_A"])
        B.op("dve", lambda E, par=par, St=St: E.tensor_scalar(
            x1b[par].ap()[:, 1536:2048], psB.ap()[:, 0:512], St.rstd, St.nmr,
            ALU.mult, ALU.add),
            reads=["ps_B", St.k["rstd"], St.k["nmr"]], writes=[f"x1b{par}", "ps_B"])
        B.op("pool", lambda E, par=par: E.tensor_tensor(
            x1b[par].ap()[:, :], x1b[par].ap()[:, :], xob[par].ap()[:, :], ALU.add),
            reads=[f"x1b{par}", f"xob{par}"], writes=[f"x1b{par}"])
        B.dma("act", f"q_x1{par}", lambda E, t=t, par=par: E.dma_start(
            out=x1_dram[128 * t:128 * (t + 1), :], in_=x1b[par].ap()[:, :]),
            reads=[f"x1b{par}"], writes=[f"x1d_{t}"])
        B.op("dve", lambda E, par=par, sqa=sqa, St=St: E.tensor_scalar(
            sqa[:, :], x1b[par].ap()[:, :], 1.0, None, ALU.mult, ALU.add,
            accum_out=St.sum),
            reads=[f"x1b{par}", sqk], writes=[sqk, St.k["sum"]])
        B.op("act", lambda E, par=par, sqa=sqa, St=St: E.activation(
            sqa[:, :], x1b[par].ap()[:, :], AF.Square, accum_out=St.sq),
            reads=[f"x1b{par}", sqk], writes=[sqk, St.k["sq"]])
        ln_finalize(St)
        B.op("act", lambda E, par=par, St=St: E.activation(
            xl2t.ap()[:, :], x1b[par].ap()[:, :], AF.Identity, bias=St.nmr,
            scale=St.rstd),
            reads=[f"x1b{par}", St.k["rstd"], St.k["nmr"]], writes=["xl2t"])
        for half in range(2):
            for i in range(8):
                k = 8 * half + i
                B.op("pe", lambda E, k=k, i=i: E.transpose(
                    psD_bf[:, 128 * i:128 * (i + 1)],
                    xl2t.ap()[:, 128 * k:128 * (k + 1)], ident.ap()[:, :]),
                    reads=["xl2t", "ident", "ps_d"], writes=["ps_d"])
            dst = xln2fm.ap()[:, :].rearrange("p (k n) -> p k n", k=16)[
                :, 8 * half:8 * (half + 1), 128 * t:128 * (t + 1)]
            B.op("dve", lambda E, dst=dst: E.tensor_copy(
                dst, psD_bf[:, :].rearrange("p (k n) -> p k n", k=8)),
                reads=["ps_d"], writes=[f"xln2fm_{t}", "ps_d"])
    xln2_all = [f"xln2fm_{t}" for t in range(OCH)]
    if "d_x1" in dbg_out:
        B.dma("act", "q_dbg", lambda E: E.dma_start(
            out=dbg_out["d_x1"], in_=x1_dram[:, :]),
            reads=[f"x1d_{t}" for t in range(OCH)])
    if "d_xln2fm" in dbg_out:
        B.dma("act", "q_dbg", lambda E: E.dma_start(
            out=dbg_out["d_xln2fm"], in_=xln2fm.ap()[:, :]), reads=xln2_all)

    if not phase_on("p4"):
        _finalize_build(nc, B)
        return _emit(nc, B)
    B.dma("pool", "q_gate", lambda E: E.dma_start(
        out=gsb.ap()[:, :].rearrange("p (k f) -> p k f", k=16), in_=ap["gate_t"]),
        writes=["gate_sb"])
    x2v = xln2fm.ap()[:, :].rearrange("p (k n) -> p k n", k=16)
    for t in range(OCH):
        for k in range(16):
            B.op("pe", lambda E, t=t, k=k: E.matmul(
                psC.ap()[:, 0:NEXP], x2v[:, k, 128 * t:128 * (t + 1)],
                gsb.ap()[:, NEXP * k:NEXP * (k + 1)],
                start=(k == 0), stop=(k == 15)),
                reads=xln2_all + ["gate_sb", "ps_ctx"], writes=["ps_ctx"])
        St = STATS[t % 2]
        B.op("dve", lambda E: E.tensor_copy(lg_sb.ap()[:, 0:8], psC.ap()[:, 0:8]),
             reads=["ps_ctx"], writes=["lg", "ps_ctx"])
        B.op("dve", lambda E, St=St: E.reduce_max(St.t0, lg_sb.ap()[:, 0:8], axis=AX),
             reads=["lg"], writes=[St.k["t0"]])
        B.op("dve", lambda E, St=St: E.tensor_scalar(
            lg_sb.ap()[:, 8:16], lg_sb.ap()[:, 0:8], St.t0, None, ALU.is_ge),
            reads=["lg", St.k["t0"]], writes=["ge"])
        B.op("dve", lambda E: E.scalar_tensor_tensor(
            lg_sb.ap()[:, 16:24], lg_sb.ap()[:, 8:16], c_neg, lg_sb.ap()[:, 0:8],
            ALU.mult, ALU.add),
            reads=["ge", "lg", "consts"], writes=["l2"])
        B.op("dve", lambda E, St=St: E.reduce_max(St.t1, lg_sb.ap()[:, 16:24], axis=AX),
             reads=["l2"], writes=[St.k["t1"]])
        B.op("dve", lambda E, t=t, St=St: E.tensor_tensor(
            top2c.ap()[:, t:t + 1], St.t0, St.t1, ALU.add),
            reads=[St.k["t0"], St.k["t1"]], writes=["top2c"])
    if "d_top2" in dbg_out:
        B.dma("act", "q_dbg", lambda E: E.dma_start(
            out=dbg_out["d_top2"], in_=top2c.ap()[:, :]), reads=["top2c"])
    B.op("pe", lambda E: E.transpose(
        psD.ap()[0:8, 0:128], top2c.ap()[:, :], identf.ap()[:, :]),
        reads=["top2c", "identf", "ps_d"], writes=["ps_d"])
    B.op("dve", lambda E: E.tensor_copy(t2tmp.ap()[:, :], psD.ap()[0:8, 0:128]),
         reads=["ps_d"], writes=["t2tmp", "ps_d"])
    for par in range(2):
        B.wait("sync", B.dma_sem(f"q_ctxT{par}"),
               16 * B.dma_count.get(f"q_ctxT{par}", 0))
    B.dma("sync", "q_s2s", lambda E: E.dma_start(
        out=sr2.ap()[0:1, :], in_=t2tmp.ap()[:, :]),
        reads=["t2tmp"], writes=["sr2"])
    for n in range(2):
        B.op("pe", lambda E, n=n: E.matmul(
            psC.ap()[:, 0:512], onesf.ap()[:, :],
            sr2.ap()[0:1, 512 * n:512 * (n + 1)], start=True, stop=True),
            reads=["sr2", "onesf", "ps_ctx"], writes=["ps_ctx"])
        B.op("act", lambda E, n=n: E.activation(
            bc2.ap()[:, 512 * n:512 * (n + 1)], psC.ap()[:, 0:512], AF.Copy),
            reads=["ps_ctx"], writes=["bc2", "ps_ctx"])

    if not phase_on("p5"):
        _finalize_build(nc, B)
        return _emit(nc, B)
    slots = [("sl0", psA.ap()[:, 0:512]), ("sl1", psA.ap()[:, 512:1024]),
             ("sl2", psA.ap()[:, 1024:1536]), ("sl3", psB.ap()[:, 0:512]),
             ("sl4", psB.ap()[:, 512:1024]), ("sl5", psB.ap()[:, 1024:1536])]
    B.fence()
    for eg in range(32):
        wpar = eg % 2
        wk = f"wup{wpar}"
        B.dma("pool", f"q_up{wpar}", lambda E, eg=eg, wpar=wpar: E.dma_start(
            out=wup[wpar].ap()[:, :], in_=ap["up_t"][eg].rearrange(
                "p k i f -> p (k i f)")),
            writes=[wk])
        for i in range(2):
            e = 2 * eg + i
            s0k, s0 = slots[(2 * e) % 6]
            s1k, s1 = slots[(2 * e + 1) % 6]
            for k in range(16):
                for n in range(2):
                    sl, slk = (s0, s0k) if n == 0 else (s1, s1k)
                    B.op("pe", lambda E, k=k, i=i, n=n, sl=sl, wpar=wpar: E.matmul(
                        sl, wup[wpar].ap()[:, (2 * k + i) * 128:(2 * k + i + 1) * 128],
                        xln2fm.ap()[:, 1024 * k + 512 * n:1024 * k + 512 * (n + 1)],
                        start=(k == 0), stop=(k == 15)),
                        reads=[wk] + xln2_all + [slk], writes=[slk])
            for n in range(2):
                sl, slk = (s0, s0k) if n == 0 else (s1, s1k)
                B.op("dve", lambda E, n=n, sl=sl: E.tensor_scalar(
                    rsc.ap()[:, 512 * n:512 * (n + 1)], sl, 0.0, None, ALU.max),
                    reads=[slk], writes=["rsc", slk])
            B.op("act", lambda E, e=e: E.activation(
                H_sb.ap()[:, 1024 * e:1024 * (e + 1)], rsc.ap()[:, :], AF.Square),
                reads=["rsc"], writes=[f"H_{e}"])
    H_all = [f"H_{e}" for e in range(64)]
    if "d_h" in dbg_out:
        B.dma("act", "q_dbg", lambda E: E.dma_start(
            out=dbg_out["d_h"], in_=H_sb.ap()[:, :]), reads=H_all)

    for k in range(16):
        B.op("dve", lambda E, k=k: E.tensor_tensor(
            xln2fm.ap()[:, 1024 * k:1024 * (k + 1)],
            xln2fm.ap()[:, 1024 * k:1024 * (k + 1)], bc2.ap()[:, :], ALU.mult),
            reads=xln2_all + ["bc2"],
            writes=[f"x2s_{k}"] + (xln2_all if k == 0 else []))
    x2s_all = [f"x2s_{k}" for k in range(16)]

    if not phase_on("p6"):
        _finalize_build(nc, B)
        return _emit(nc, B)
    if "d_bc2" in dbg_out:
        B.dma("act", "q_dbg", lambda E: E.dma_start(
            out=dbg_out["d_bc2"], in_=bc2.ap()[:, :]), reads=["bc2"])
    if "d_x2s" in dbg_out:
        B.dma("act", "q_dbg", lambda E: E.dma_start(
            out=dbg_out["d_x2s"], in_=xln2fm.ap()[:, :]), reads=x2s_all)

    for dg in range(8):
        dpair = []
        for i in range(2):
            d = 2 * dg + i
            s0k, s0 = slots[(2 * d) % 6]
            s1k, s1 = slots[(2 * d + 1) % 6]
            dpair.append(((s0k, s0), (s1k, s1)))
        for sub in range(8):
            wpar = sub % 2
            wk = f"wdn{wpar}"
            B.dma("pool", f"q_dn{wpar}", lambda E, dg=dg, sub=sub, wpar=wpar:
                  E.dma_start(out=wdn[wpar].ap()[:, :],
                              in_=ap["down_t"][dg][:, 8 * sub:8 * (sub + 1)]
                              .rearrange("p k i f -> p (k i f)")),
                  writes=[wk])
            for k in range(8 * sub, 8 * sub + 8):
                koff = k % 8
                for i in range(2):
                    (s0k, s0), (s1k, s1) = dpair[i]
                    for n in range(2):
                        sl, slk = (s0, s0k) if n == 0 else (s1, s1k)
                        B.op("pe", lambda E, koff=koff, i=i, n=n, sl=sl,
                             wpar=wpar, k=k: E.matmul(
                            sl, wdn[wpar].ap()[:, (2 * koff + i) * 128:(2 * koff + i + 1) * 128],
                            H_sb.ap()[:, 1024 * k + 512 * n:1024 * k + 512 * (n + 1)],
                            start=(k == 0), stop=False),
                            reads=[wk] + H_all + [slk], writes=[slk])
        for sub in range(2):
            B.dma("pool", f"q_dn{sub}", lambda E, dg=dg, sub=sub: E.dma_start(
                out=wdn[sub].ap()[:, 0:2048],
                in_=ap["exp_t"][dg][:, 8 * sub:8 * (sub + 1)]
                .rearrange("p k i f -> p (k i f)")),
                writes=[f"wex{sub}", f"wdn{sub}"])
            for k in range(8 * sub, 8 * sub + 8):
                koff = k % 8
                for i in range(2):
                    (s0k, s0), (s1k, s1) = dpair[i]
                    for n in range(2):
                        sl, slk = (s0, s0k) if n == 0 else (s1, s1k)
                        B.op("pe", lambda E, koff=koff, i=i, n=n, sl=sl,
                             sub=sub, k=k: E.matmul(
                            sl, wdn[sub].ap()[:, (2 * koff + i) * 128:(2 * koff + i + 1) * 128],
                            xln2fm.ap()[:, 1024 * k + 512 * n:1024 * k + 512 * (n + 1)],
                            start=False, stop=(k == 15)),
                            reads=[f"wdn{sub}"] + x2s_all + [slk], writes=[slk])
        for i in range(2):
            d = 2 * dg + i
            ypar = i
            (s0k, s0), (s1k, s1) = dpair[i]
            B.op("dve", lambda E, s0=s0, ypar=ypar: E.tensor_copy(
                ysc[ypar].ap()[:, 0:512], s0),
                reads=[s0k], writes=[f"ysc{ypar}", s0k, "rsc"])
            B.op("dve", lambda E, s1=s1, ypar=ypar: E.tensor_copy(
                ysc[ypar].ap()[:, 512:1024], s1), reads=[s1k], writes=[f"ysc{ypar}", s1k])
            for half in range(2):
                for j in range(4):
                    tc = 4 * half + j
                    B.op("pe", lambda E, tc=tc, j=j, ypar=ypar: E.transpose(
                        psC.ap()[:, 128 * j:128 * (j + 1)],
                        ysc[ypar].ap()[:, 128 * tc:128 * (tc + 1)],
                        identf.ap()[:, :]),
                        reads=[f"ysc{ypar}", "identf", "ps_ctx"], writes=["ps_ctx"])
                B.op("dve", lambda E, half=half: E.tensor_copy(
                    yst.ap()[:, 512 * half:512 * (half + 1)], psC.ap()[:, :]),
                    reads=["ps_ctx"], writes=["yst", "ps_ctx"])
            B.dma("act", "q_y", lambda E, d=d: E.dma_start(
                out=y_dram[d].rearrange("c p f -> p c f"),
                in_=yst.ap()[:, :].rearrange("p (c f) -> p c f", c=8)),
                reads=["yst"], writes=[f"yd_{d}"])
    if "d_y" in dbg_out:
        B.dma("act", "q_dbg", lambda E: E.dma_start(
            out=dbg_out["d_y"].rearrange("d c p f -> (d c) (p f)"),
            in_=y_dram.rearrange("d c p f -> (d c) (p f)")),
            reads=[f"yd_{d}" for d in range(16)])

    if not phase_on("p7"):
        _finalize_build(nc, B)
        return _emit(nc, B)
    B.fence()
    B.dma_group("sync", "q_rows", [
        lambda E: E.dma_start(out=grow.ap()[:, :], in_=ap["lng"]),
        lambda E: E.dma_start(out=browt.ap()[:, :], in_=ap["lnb"]),
    ], writes=["grow", "brw"])
    for n in range(4):
        B.op("pe", lambda E, n=n: E.matmul(
            psD.ap()[:, 0:512], onesf.ap()[:, :],
            grow.ap()[0:1, 512 * n:512 * (n + 1)], start=True, stop=True),
            reads=["grow", "onesf", "ps_d"], writes=["ps_d"])
        B.op("act", lambda E, n=n: E.activation(
            gamb.ap()[:, 512 * n:512 * (n + 1)], psD.ap()[:, 0:512], AF.Copy),
            reads=["ps_d"], writes=["gamb", "ps_d"])
    for n in range(4):
        B.op("pe", lambda E, n=n: E.matmul(
            psD.ap()[:, 0:512], onesf.ap()[:, :],
            browt.ap()[0:1, 512 * n:512 * (n + 1)], start=True, stop=True),
            reads=["brw", "onesf", "ps_d"], writes=["ps_d"])
        B.op("act", lambda E, n=n: E.activation(
            betb.ap()[:, 512 * n:512 * (n + 1)], psD.ap()[:, 0:512], AF.Copy),
            reads=["ps_d"], writes=["betb", "ps_d"])
    def p7_load(t):
        par = t % 3
        ybk, x1k = f"yb{par}", f"x1r{par}"
        B.dma("sync", f"q_yb{par}", lambda E, t=t, par=par: E.dma_start(
            out=yb[par].ap()[:, :].rearrange("p (d f) -> p d f", d=16),
            in_=y_dram[:, t].rearrange("d p f -> p d f")),
            reads=[f"yd_{d}" for d in range(16)], writes=[ybk])
        B.dma("sync", f"q_x1r{par}", lambda E, t=t, par=par: E.dma_start(
            out=x1r[par].ap()[:, :], in_=x1_dram[128 * t:128 * (t + 1), :]),
            reads=[f"x1d_{t}"], writes=[x1k])

    def p7_a(t):
        par = t % 3
        spar = t % 2
        ybk = f"yb{par}"
        ya = yb[par].ap()
        St = STATS[spar]
        sqa = sq3[spar].ap()
        sqk = f"sq3_{spar}"
        B.op("dve", lambda E, ya=ya, sqa=sqa, St=St: E.tensor_scalar(
            sqa[:, :], ya[:, :], 1.0, None, ALU.mult, ALU.add, accum_out=St.sum),
            reads=[ybk, sqk], writes=[sqk, St.k["sum"]])
        B.op("act", lambda E, ya=ya, sqa=sqa, St=St: E.activation(
            sqa[:, :], ya[:, :], AF.Square, accum_out=St.sq),
            reads=[ybk, sqk], writes=[sqk, St.k["sq"]])
        ln_finalize(St)
        B.op("dve", lambda E, ya=ya, St=St: E.tensor_scalar(
            ya[:, :], ya[:, :], St.rstd, St.nmr, ALU.mult, ALU.add),
            reads=[ybk, St.k["rstd"], St.k["nmr"]], writes=[ybk])
        B.op("pool", lambda E, ya=ya: E.tensor_tensor(
            ya[:, :], ya[:, :], gamb.ap()[:, :], ALU.mult),
            reads=[ybk, "gamb"], writes=[ybk])
        B.op("pool", lambda E, ya=ya: E.tensor_tensor(
            ya[:, :], ya[:, :], betb.ap()[:, :], ALU.add),
            reads=[ybk, "betb"], writes=[ybk])

    def p7_b(t):
        par = t % 3
        spar = t % 2
        ybk, x1k = f"yb{par}", f"x1r{par}"
        ya = yb[par].ap()
        St = STATS[spar]
        sqa = sq3[spar].ap()
        sqk = f"sq3_{spar}"
        B.op("dve", lambda E, ya=ya, sqa=sqa, St=St: E.tensor_scalar(
            sqa[:, :], ya[:, :], 1.0, None, ALU.mult, ALU.add, accum_out=St.sum),
            reads=[ybk, sqk], writes=[sqk, St.k["sum"]])
        B.op("act", lambda E, ya=ya, sqa=sqa, St=St: E.activation(
            sqa[:, :], ya[:, :], AF.Square, accum_out=St.sq),
            reads=[ybk, sqk], writes=[sqk, St.k["sq"]])
        ln_finalize(St)
        obk = f"ob{par}"
        B.op("dve", lambda E, ya=ya, St=St, par=par: E.tensor_scalar(
            ob[par].ap()[:, :], ya[:, :], St.rstd, St.nmr, ALU.mult, ALU.add),
            reads=[ybk, St.k["rstd"], St.k["nmr"]], writes=[obk])
        B.op("pool", lambda E, par=par: E.tensor_tensor(
            ob[par].ap()[:, :], ob[par].ap()[:, :], x1r[par].ap()[:, :], ALU.add),
            reads=[obk, x1k], writes=[obk])
        B.dma("act", f"q_out{par}", lambda E, t=t, par=par: E.dma_start(
            out=out[128 * t:128 * (t + 1), :], in_=ob[par].ap()[:, :]),
            reads=[obk], writes=[f"out_{t}"])

    p7_load(0)
    p7_load(1)
    p7_load(2)
    p7_a(0)
    for t in range(OCH):
        if t + 1 < OCH:
            p7_a(t + 1)
        p7_b(t)
        if t + 3 < OCH:
            p7_load(t + 3)

    _finalize_build(nc, B)
    return _emit(nc, B)


def _finalize_build(nc, B):
    for qo in ("q_out0", "q_out1", "q_out2"):
        if qo in B.dma_count:
            B.wait("sync", B.dma_sem(qo), 16 * B.dma_count[qo])
    else:
        pass
    for qd in ("q_dbg",):
        if qd in B.dma_count:
            B.wait("sync", qd, 16 * B.dma_count[qd])
    B.fence()
    for s in list(B.nofence_sems):
        if B.dma_count.get(s, 0) > 0:
            B.wait("sync", s, B.dma_count[s])

def _emit(nc, B):
    import contextlib
    with contextlib.ExitStack() as stack:
        for s in B.sem_names:
            B.sems[s] = stack.enter_context(nc.semaphore(s))
        block = stack.enter_context(nc.Block())
        B.replay(block)
    return nc


_CACHED = {}


def kernel(L0_attn_x, L0_attn_Wq, L0_attn_Wk, L0_attn_Wv, L0_attn_Wo,
           L0_ffn_gate_w, L0_ffn_up_w, L0_ffn_down_w, L0_ffn_expert_weight,
           L0_ffn_ln_gamma, L0_ffn_ln_beta, _debug=(), _nc=None, _upto="p7"):
    args = [np.asarray(a, dtype=np.float32) for a in (
        L0_attn_x, L0_attn_Wq, L0_attn_Wk, L0_attn_Wv, L0_attn_Wo,
        L0_ffn_gate_w, L0_ffn_up_w, L0_ffn_down_w, L0_ffn_expert_weight,
        L0_ffn_ln_gamma, L0_ffn_ln_beta)]
    shared = host_shared(*args)
    in_maps = [host_inputs(c, *args, shared) for c in range(NCORES)]
    key = (tuple(sorted(_debug)), _upto)
    if _nc is not None:
        nc = _nc
    else:
        if key not in _CACHED:
            _CACHED[key] = build_program(debug=_debug, upto=_upto)
        nc = _CACHED[key]
    res = run_bass_kernel_spmd(nc, in_maps, core_ids=list(range(NCORES)))
    outs = np.concatenate([res.results[c]["out"] for c in range(NCORES)], axis=0)
    if _debug:
        return outs.reshape(1, S, D), res
    return outs.reshape(1, S, D)


# version 07
# speedup vs baseline: 1.1485x; 1.1485x over previous
import math
import numpy as np
import ml_dtypes

import concourse.bass as bass
import concourse.mybir as mybir
from concourse.bass_utils import run_bass_kernel_spmd
from concourse.bass_core_rust import IndirectOffsetOnAxis

F32 = mybir.dt.float32
BF16 = mybir.dt.bfloat16
I32 = mybir.dt.int32
AF = mybir.ActivationFunctionType
ALU = mybir.AluOpType
AX = mybir.AxisListType.X

S, D, H_HEADS, HD, INTER, NEXP = 8192, 2048, 16, 128, 8192, 8
NCORES = 8
TOK = S // NCORES
NCH = S // 128
OCH = TOK // 128
SHIFT = 12.0
SCALE = 1.0 / math.sqrt(HD)
EPS = 1e-5

ENGINES = ("pe", "act", "dve", "pool", "sync")
P2_HEADS = 2
P2_QTILES = 16
ENG_SEM = {e: "E_" + e for e in ENGINES}

C_EPS, C_INVD, C_SCALE, C_NSHIFT, C_NEG1E30 = 0, 1, 2, 3, 4
NCONST = 8


class Builder:
    def __init__(self):
        self.ops = {e: [] for e in ENGINES}
        self.sem_names = [ENG_SEM[e] for e in ENGINES]
        self.eng_count = {e: 0 for e in ENGINES}
        self.dma_count = {}
        self.bufs = {}
        self.waited = {e: {} for e in ENGINES}
        self.sems = {}
        self.nofence_sems = set()

    def dma_sem(self, name):
        if name not in self.dma_count:
            self.dma_count[name] = 0
            self.sem_names.append(name)
        return name

    def _dep_map(self, reads, writes):
        deps = {}
        for b in reads:
            e = self.bufs.get(b)
            if e and e["w"]:
                s, v = e["w"]
                deps[s] = max(deps.get(s, 0), v)
        for b in writes:
            e = self.bufs.get(b)
            if e:
                if e["w"]:
                    s, v = e["w"]
                    deps[s] = max(deps.get(s, 0), v)
                for s, v in e["r"]:
                    deps[s] = max(deps.get(s, 0), v)
        return deps

    def _emit_waits(self, eng, deps, own_sem):
        for s, v in sorted(deps.items()):
            if s == own_sem and eng == "pe":
                continue
            self.wait(eng, s, v)

    def wait(self, eng, sem, val):
        if val <= 0 or self.waited[eng].get(sem, 0) >= val:
            return
        self.waited[eng][sem] = val
        self.ops[eng].append(("w", sem, val))

    def op(self, eng, fn, reads=(), writes=()):
        own = ENG_SEM[eng]
        self._emit_waits(eng, self._dep_map(reads, writes), own)
        self.eng_count[eng] += 1
        val = self.eng_count[eng]
        self.ops[eng].append(("i", fn, own))
        for b in reads:
            self.bufs.setdefault(b, {"w": None, "r": []})["r"].append((own, val))
        for b in writes:
            self.bufs[b] = {"w": (own, val), "r": []}
        return (own, val)

    def dma(self, eng, sem, fn, reads=(), writes=()):
        sem = self.dma_sem(sem)
        self._emit_waits(eng, self._dep_map(reads, writes), None)
        self.dma_count[sem] += 1
        val = 16 * self.dma_count[sem]
        self.ops[eng].append(("d", fn, sem))
        for b in reads:
            self.bufs.setdefault(b, {"w": None, "r": []})["r"].append((sem, val))
        for b in writes:
            self.bufs[b] = {"w": (sem, val), "r": []}
        return (sem, val)

    def dma_group(self, eng, sem, fns, reads=(), writes=()):
        sem = self.dma_sem(sem)
        self._emit_waits(eng, self._dep_map(reads, writes), None)
        for fn in fns:
            self.dma_count[sem] += 1
            self.ops[eng].append(("d", fn, sem))
        val = 16 * self.dma_count[sem]
        for b in reads:
            self.bufs.setdefault(b, {"w": None, "r": []})["r"].append((sem, val))
        for b in writes:
            self.bufs[b] = {"w": (sem, val), "r": []}
        return (sem, val)

    def collective(self, fn, sem, reads=(), writes=()):
        sem = self.dma_sem(sem)
        self.nofence_sems.add(sem)
        self._emit_waits("pool", self._dep_map(reads, writes), None)
        self.dma_count[sem] += 1
        val = self.dma_count[sem]
        self.ops["pool"].append(("c", fn, sem))
        for b in writes:
            self.bufs[b] = {"w": (sem, val), "r": []}
        return (sem, val)

    def fence(self):
        snap_eng = dict(self.eng_count)
        snap_dma = {s: 16 * c for s, c in self.dma_count.items()
                    if s not in self.nofence_sems}
        for e in ENGINES:
            for oe in ENGINES:
                if oe != e and snap_eng[oe] > 0:
                    self.wait(e, ENG_SEM[oe], snap_eng[oe])
            for s, v in snap_dma.items():
                if v > 0:
                    self.wait(e, s, v)

    def replay(self, block):
        sems = self.sems

        def mk(eng_name):
            def body(E):
                for item in self.ops[eng_name]:
                    kind = item[0]
                    if kind == "w":
                        E.wait_ge(sems[item[1]], item[2])
                    elif kind == "i":
                        item[1](E).then_inc(sems[item[2]], 1)
                    elif kind == "d":
                        item[1](E).then_inc(sems[item[2]], 16)
                    else:
                        item[1](E).then_inc(sems[item[2]], 1)
            return body

        block.tensor(mk("pe"))
        block.scalar(mk("act"))
        block.vector(mk("dve"))
        block.gpsimd(mk("pool"))
        block.sync(mk("sync"))


def host_inputs(core, x, Wq, Wk, Wv, Wo, gate_w, up_w, down_w, expert_w,
                ln_gamma, ln_beta, shared):
    c = core
    m = {}
    m["xf"] = shared["xf"]
    m["xo"] = np.ascontiguousarray(shared["xf"][TOK * c:TOK * (c + 1)])
    hc = slice(256 * c, 256 * (c + 1))
    wqk = np.concatenate([Wq[:, hc], Wk[:, hc]], axis=1)
    m["wqk_t"] = np.ascontiguousarray(wqk.reshape(16, 128, 512).transpose(1, 0, 2))
    m["wv_t"] = np.ascontiguousarray(
        Wv[:, hc].reshape(16, 128, 256).transpose(1, 0, 2))
    m.update(shared["static"])
    idx = np.zeros((128, 1), np.int32)
    for p in range(128):
        r, i = p // 16, p % 16
        idx[p, 0] = r * S + TOK * c + 64 * i
    m["ctx_idx"] = idx
    return m


def host_shared(x, Wq, Wk, Wv, Wo, gate_w, up_w, down_w, expert_w,
                ln_gamma, ln_beta):
    xf = np.ascontiguousarray(np.asarray(x).reshape(S, D).astype(np.float32))
    st = {}
    st["wo_t"] = np.ascontiguousarray(
        np.asarray(Wo).reshape(16, 128, 2048).transpose(1, 0, 2))
    st["gate_t"] = np.ascontiguousarray(
        np.asarray(gate_w).reshape(16, 128, 8).transpose(1, 0, 2))
    st["up_t"] = np.ascontiguousarray(
        np.asarray(up_w).reshape(16, 128, 32, 2, 128).transpose(2, 1, 0, 3, 4))
    st["down_t"] = np.ascontiguousarray(
        np.asarray(down_w).reshape(64, 128, 8, 2, 128).transpose(2, 1, 0, 3, 4))
    st["exp_t"] = np.ascontiguousarray(
        np.asarray(expert_w).reshape(16, 128, 8, 2, 128).transpose(2, 1, 0, 3, 4))
    st["lng"] = np.ascontiguousarray(np.asarray(ln_gamma).reshape(1, D).astype(np.float32))
    st["lnb"] = np.ascontiguousarray(np.asarray(ln_beta).reshape(1, D).astype(np.float32))
    freqs = 1.0 / 10000.0 ** (np.arange(0, HD, 2, dtype=np.float32) / HD)
    ang = np.outer(np.arange(S, dtype=np.float32), freqs)
    st["cos"] = np.ascontiguousarray(np.tile(np.cos(ang), (1, 2)).astype(np.float32))
    st["sin"] = np.ascontiguousarray(np.tile(np.sin(ang), (1, 2)).astype(np.float32))
    mk = np.zeros((4, 128, 512), np.float32)
    for b in range(4):
        mk[b] = (128 * b + np.arange(128)[:, None]) <= np.arange(512)[None, :]
    st["masks"] = np.ascontiguousarray(
        mk.reshape(512, 512).astype(ml_dtypes.bfloat16))
    st["ident"] = np.eye(128, dtype=ml_dtypes.bfloat16)
    st["ident_f"] = np.eye(128, dtype=np.float32)
    cons = np.zeros((128, NCONST), np.float32)
    cons[:, C_EPS] = EPS
    cons[:, C_INVD] = 1.0 / D
    cons[:, C_SCALE] = SCALE
    cons[:, C_NSHIFT] = -SHIFT
    cons[:, C_NEG1E30] = -1e30
    st["consts"] = cons
    st["ones_bf"] = np.ones((128, 1), ml_dtypes.bfloat16)
    st["ones_f"] = np.ones((1, 128), np.float32)
    return {"xf": xf, "static": st}


PHASES = ["p1", "p2", "gather", "p3", "p4", "p5", "p6", "p7"]


def build_program(debug=(), upto="p7"):
    _lim = PHASES.index(upto)
    def phase_on(name):
        return PHASES.index(name) <= _lim
    nc = bass.Bass("TRN2", target_bir_lowering=False, debug=False)
    dbg = set(debug)
    B = Builder()

    ap = {}
    def din(name, shape, dtype=F32):
        ap[name] = nc.dram_tensor(name, list(shape), dtype, kind="ExternalInput").ap()
    din("xf", (S, D)); din("xo", (TOK, D))
    din("wqk_t", (128, 16, 512)); din("wv_t", (128, 16, 256))
    din("wo_t", (128, 16, 2048)); din("gate_t", (128, 16, 8))
    din("up_t", (32, 128, 16, 2, 128))
    din("down_t", (8, 128, 64, 2, 128))
    din("exp_t", (8, 128, 16, 2, 128))
    din("lng", (1, D)); din("lnb", (1, D))
    din("cos", (S, 128)); din("sin", (S, 128))
    din("masks", (512, 512), BF16); din("ident", (128, 128), BF16)
    din("ident_f", (128, 128), F32)
    din("consts", (128, NCONST)); din("ones_bf", (128, 1), BF16)
    din("ones_f", (1, 128)); din("ctx_idx", (128, 1), I32)

    out = nc.dram_tensor("out", [TOK, D], F32, kind="ExternalOutput").ap()
    dbg_out = {}
    def ddbg(name, shape, dt=F32):
        if name in dbg:
            dbg_out[name] = nc.dram_tensor(name, list(shape), dt,
                                           kind="ExternalOutput").ap()
    ddbg("d_qkvfm", (6, 128, S), BF16)
    ddbg("d_scr", (128, 768))
    ddbg("d_rope", (128, 768), BF16)
    ddbg("d_cs", (128, 256))
    ddbg("d_ctxtm", (2, 128, S), BF16)
    ddbg("d_vtm", (128, S), BF16)
    ddbg("d_pb", (128, 3 * 1536), BF16)
    ddbg("d_den", (1, 512))
    ddbg("d_cn", (128, 512), BF16)
    ddbg("d_bc", (128, 1024))
    ddbg("d_ctxown", (8, 2, TOK, 128), BF16)
    ddbg("d_x1", (TOK, D))
    ddbg("d_xln2fm", (128, 16 * TOK), BF16)
    ddbg("d_top2", (128, OCH))
    ddbg("d_h", (128, 64 * TOK), BF16)
    ddbg("d_bc2", (128, 1024), BF16)
    ddbg("d_x2s", (128, 16 * TOK), BF16)
    ddbg("d_y", (16, OCH, 128, 128))

    cc_in = [nc.dram_tensor(f"cc_in{h}", [S, 128], BF16).ap() for h in range(2)]
    cc_out = [nc.dram_tensor(f"cc_out{h}", [NCORES * S, 128], BF16,
                             addr_space="Shared").ap()
              for h in range(2)]
    ctx_own = nc.dram_tensor("ctx_own", [8, 2, TOK, 128], BF16).ap()
    x1_dram = nc.dram_tensor("x1_dram", [TOK, D], F32).ap()
    y_dram = nc.dram_tensor("y_dram", [16, OCH, 128, 128], F32).ap()

    SB_TOP = nc.sbuf_top - (nc.sbuf_top % 32)

    def sb(name, shape, dtype, off):
        per = int(np.prod(shape[1:])) * mybir.dt.size(dtype)
        assert off % 32 == 0 and off + per <= SB_TOP, (name, off, per, SB_TOP)
        return nc.alloc_sbuf_tensor_at(name, list(shape), dtype, offset=off)

    PB = (nc.sbuf_base + 31) // 32 * 32
    BASE0 = PB
    consts = sb("consts", [128, NCONST], F32, PB); PB += 32
    ident = sb("ident", [128, 128], BF16, PB); PB += 256
    identf = sb("identf", [128, 128], F32, PB); PB += 512
    onesb = sb("onesb", [128, 1], BF16, PB); PB += 32
    onesf = sb("onesf", [1, 128], F32, PB); PB += 512
    idxsb = sb("idxsb", [128, 1], I32, PB); PB += 32
    small = sb("small", [128, 32], F32, PB); PB += 128
    rsq = sb("rsq", [128, 1024], F32, PB); PB += 4096
    assert PB <= BASE0 + 8192
    A0 = BASE0 + 8192
    A1 = A0 + 96 * 1024
    B0 = A1
    XL2 = SB_TOP - 32 * 1024

    qkvfm = sb("qkvfm", [128, 6 * S], BF16, A0)

    o = B0
    xt = [sb(f"xt{i}", [128, D], F32, o + i * 8192) for i in range(2)]; o += 16384
    sqs = sb("sqs", [128, D], BF16, o); o += 4096
    xln1 = [sb(f"xln1_{i}", [128, D], BF16, o + i * 4096) for i in range(2)]; o += 8192
    xfm = [sb(f"xfm{i}", [128, D], BF16, o + i * 4096) for i in range(2)]; o += 8192
    qkvscr = [sb(f"qkvscr{i}", [128, 768], F32, o + i * 3072) for i in range(2)]
    o += 6144
    qkvbf = [sb(f"qkvbf{i}", [128, 768], BF16, o + i * 1536) for i in range(2)]
    o += 3072
    cost = [sb(f"cost{i}", [128, 128], F32, o + i * 512) for i in range(2)]; o += 1024
    sint = [sb(f"sint{i}", [128, 128], F32, o + i * 512) for i in range(2)]; o += 1024
    wqk_sb = sb("wqk_sb", [128, 16 * 512], BF16, o); o += 16384
    wv_sb = sb("wv_sb", [128, 16 * 256], BF16, o); o += 8192
    assert o <= XL2

    o = B0
    vtm = sb("vtm", [128, S], BF16, o); o += 16384
    ctxtm = [sb(f"ctxtm{h}", [128, S], BF16, o + h * 16384) for h in range(2)]; o += 32768
    masks_sb = sb("masks_sb", [128, 2048], BF16, o); o += 4096
    pb = [sb(f"pb{i}", [128, 1536], BF16, o + i * 3072) for i in range(3)]; o += 9216
    padd = [sb(f"padd{i}", [128, 512], BF16, o + i * 1024) for i in range(2)]; o += 2048
    cnsb = sb("cnsb", [128, 512], BF16, o); o += 1024
    bcsb = sb("bcsb", [128, 1024], F32, o); o += 4096
    srow = sb("srow", [1, 512], F32, o); o += 2048
    assert o <= XL2

    xln2fm = sb("xln2fm", [128, 16 * TOK], BF16, XL2)
    wo_sb = sb("wo_sb", [128, 16 * D], BF16, A0)
    gth = [sb(f"gth{h}", [128, 8192], BF16, A0 + 65536 + h * 16384) for h in range(2)]
    o = B0
    ctxT = [sb(f"ctxT{i}", [128, D], BF16, o + i * 4096) for i in range(2)]; o += 8192
    x1b = [sb(f"x1b{i}", [128, D], F32, o + i * 8192) for i in range(2)]; o += 16384
    xob = [sb(f"xob{i}", [128, D], F32, o + i * 8192) for i in range(2)]; o += 16384
    xl2t = [sb(f"xl2t{i}", [128, D], BF16, o + i * 4096) for i in range(2)]
    o += 8192
    sq3 = [sb(f"sq3_{i}", [128, D], F32, o + i * 8192) for i in range(2)]; o += 16384
    assert o <= XL2
    o4 = A0 + 131072 + 16384 + 8192 + 8192 + 4096
    lg_sb = sb("lg_sb", [128, 32], F32, o4); o4 += 128
    t2tmp = sb("t2tmp", [8, 128], F32, o4); o4 += 512
    top2c = sb("top2c", [128, OCH], F32, o4); o4 += 32
    gsb = sb("gsb", [128, 16 * NEXP], BF16, o4); o4 += 256
    bc2 = sb("bc2", [128, 1024], BF16, o4); o4 += 2048
    assert o4 <= XL2, o4
    sr2 = sb("sr2", [1, 1024], F32, A0 + 131072 + 16384 + 8192)

    H_sb = sb("H_sb", [128, 64 * TOK], BF16, A0)
    HEND = A0 + 128 * 1024
    o = HEND
    wup = [sb(f"wup{i}", [128, 4096], BF16, o + i * 8192) for i in range(2)]; o += 16384
    wdn = [sb(f"wdn{i}", [128, 2048], BF16, o + i * 4096) for i in range(2)]; o += 8192
    rsc = sb("rsc", [128, 1024], F32, o)
    ysc = [sb(f"ysc{i}", [128, 1024], F32, o + i * 4096) for i in range(2)]
    o += 8192
    yst = sb("yst", [128, 1024], F32, o); o += 4096
    assert o <= XL2

    o = A0
    gamb = sb("gamb", [128, D], F32, o); o += 8192
    betb = sb("betb", [128, D], F32, o); o += 8192
    grow = sb("grow", [1, D], F32, o); o += 8192
    browt = sb("browt", [1, D], F32, o); o += 8192
    yb = [sb(f"yb{i}", [128, D], F32, o + i * 8192) for i in range(2)]; o += 16384
    x1r = [sb(f"x1r{i}", [128, D], F32, o + i * 8192) for i in range(2)]; o += 16384
    ob = [sb(f"ob{i}", [128, D], F32, o + i * 8192) for i in range(2)]; o += 16384
    yb.append(sb("yb2", [128, D], F32, B0))
    x1r.append(sb("x1r2", [128, D], F32, B0 + 8192))
    ob.append(sb("ob2", [128, D], F32, B0 + 16384))
    assert o <= A0 + 131072

    psA = nc.alloc_psum_tensor("psA", [128, 1536], F32)
    psB = nc.alloc_psum_tensor("psB", [128, 1536], F32)
    psC = nc.alloc_psum_tensor("psC", [128, 512], F32)
    psD = nc.alloc_psum_tensor("psD", [128, 512], F32)
    psC_bf = psC.ap().bitcast(BF16)
    psD_bf = psD.ap().bitcast(BF16)

    cons = consts.ap()
    c_eps = cons[:, C_EPS:C_EPS + 1]
    c_invd = cons[:, C_INVD:C_INVD + 1]
    c_scale = cons[:, C_SCALE:C_SCALE + 1]
    c_nshift = cons[:, C_NSHIFT:C_NSHIFT + 1]
    c_neg = cons[:, C_NEG1E30:C_NEG1E30 + 1]

    sm = small.ap()

    class Stats:
        def __init__(self, par):
            base = 16 * par
            (self.sum, self.sq, self.m, self.var, self.rstd, self.nmr,
             self.t0, self.t1) = [sm[:, base + i:base + i + 1] for i in range(8)]
            self.k = {n: f"s{par}_{n}" for n in
                      ("sum", "sq", "m", "var", "rstd", "nmr", "t0", "t1")}

    STATS = [Stats(0), Stats(1)]

    def ln_finalize(S):
        k = S.k
        B.op("dve", lambda E: E.tensor_scalar(S.m, S.sum, c_invd, None, ALU.mult),
             reads=[k["sum"], "consts"], writes=[k["m"]])
        B.op("dve", lambda E: E.tensor_scalar(S.t0, S.sq, c_invd, None, ALU.mult),
             reads=[k["sq"], "consts"], writes=[k["t0"]])
        B.op("dve", lambda E: E.tensor_tensor(S.t1, S.m, S.m, ALU.mult),
             reads=[k["m"]], writes=[k["t1"]])
        B.op("dve", lambda E: E.tensor_tensor(S.var, S.t0, S.t1, ALU.subtract),
             reads=[k["t0"], k["t1"]], writes=[k["var"]])
        B.op("act", lambda E: E.activation(S.t0, S.var, AF.Sqrt, bias=c_eps),
             reads=[k["var"], "consts"], writes=[k["t0"]])
        B.op("dve", lambda E: E.reciprocal(S.rstd, S.t0),
             reads=[k["t0"]], writes=[k["rstd"]])
        B.op("dve", lambda E: E.scalar_tensor_tensor(
            S.nmr, S.m, -1.0, S.rstd, ALU.mult, ALU.mult),
            reads=[k["m"], k["rstd"]], writes=[k["nmr"]])

    _pre = [("consts", consts, "consts"), ("ident", ident, "ident"),
            ("ident_f", identf, "identf"), ("ones_bf", onesb, "onesb"),
            ("ones_f", onesf, "onesf"), ("ctx_idx", idxsb, "idxsb")]
    B.dma_group("sync", "q_pre",
                [lambda E, n=n, t=t: E.dma_start(out=t.ap()[:, :], in_=ap[n])
                 for n, t, _ in _pre],
                writes=[k for _, _, k in _pre])
    B.dma_group("pool", "q_w0", [
        lambda E: E.dma_start(
            out=wqk_sb.ap()[:, :].rearrange("p (k f) -> p k f", k=16),
            in_=ap["wqk_t"]),
        lambda E: E.dma_start(
            out=wv_sb.ap()[:, :].rearrange("p (k f) -> p k f", k=16),
            in_=ap["wv_t"]),
    ], writes=["wqk_sb", "wv_sb"])

    def p1_s1(t):
        par = t % 2
        xa = xt[par].ap()
        xk, xlk = f"xt{par}", f"xln1_{par}"
        B.dma("sync", f"q_x{par}", lambda E, t=t, xa=xa: E.dma_start(
            out=xa[:, :], in_=ap["xf"][128 * t:128 * (t + 1), :]), writes=[xk])
        B.dma_group("sync", f"q_cs{par}", [
            lambda E, t=t, p=par: E.dma_start(
                out=cost[p].ap()[:, :], in_=ap["cos"][128 * t:128 * (t + 1), :]),
            lambda E, t=t, p=par: E.dma_start(
                out=sint[p].ap()[:, :], in_=ap["sin"][128 * t:128 * (t + 1), :]),
        ], writes=[f"cost{par}", f"sint{par}"])
        St = STATS[par]
        B.op("dve", lambda E, xa=xa, St=St: E.tensor_scalar(
            xa[:, :], xa[:, :], 1.0, None, ALU.mult, ALU.add, accum_out=St.sum),
            reads=[xk], writes=[xk, St.k["sum"]])
        B.op("act", lambda E, xa=xa, St=St: E.activation(
            sqs.ap()[:, :], xa[:, :], AF.Square, accum_out=St.sq),
            reads=[xk], writes=[St.k["sq"]])
        ln_finalize(St)
        if par == 0:
            B.op("dve", lambda E, xa=xa, p=par, St=St: E.tensor_scalar(
                xln1[p].ap()[:, :], xa[:, :], St.rstd, St.nmr, ALU.mult, ALU.add),
                reads=[xk, St.k["rstd"], St.k["nmr"]], writes=[xlk])
        else:
            B.op("act", lambda E, xa=xa, p=par, St=St: E.activation(
                xln1[p].ap()[:, :], xa[:, :], AF.Identity, bias=St.nmr, scale=St.rstd),
                reads=[xk, St.k["rstd"], St.k["nmr"]], writes=[xlk])

    def p1_s2t(t):
        par = t % 2
        xlk, xfk = f"xln1_{par}", f"xfm{par}"
        pst = psD_bf if par == 0 else psC_bf
        pstk = "ps_d" if par == 0 else "ps_c"
        for half in range(2):
            for i in range(8):
                k = 8 * half + i
                B.op("pe", lambda E, p=par, k=k, i=i, pst=pst: E.transpose(
                    pst[:, 128 * i:128 * (i + 1)],
                    xln1[p].ap()[:, 128 * k:128 * (k + 1)], ident.ap()[:, :]),
                    reads=[xlk, "ident", pstk], writes=[pstk])
            eng = "dve" if (t + half) % 2 == 0 else "act"
            if eng == "dve":
                B.op("dve", lambda E, p=par, h=half, pst=pst: E.tensor_copy(
                    xfm[p].ap()[:, 1024 * h:1024 * (h + 1)], pst[:, :]),
                    reads=[pstk], writes=[xfk, pstk])
            else:
                B.op("act", lambda E, p=par, h=half, pst=pst: E.activation(
                    xfm[p].ap()[:, 1024 * h:1024 * (h + 1)], pst[:, :], AF.Copy),
                    reads=[pstk], writes=[xfk, pstk])

    def p1_s2m(t):
        par = t % 2
        xfk = f"xfm{par}"
        ps = psA.ap() if par == 0 else psB.ap()
        psk = "ps_A" if par == 0 else "ps_B"
        for k in range(16):
            B.op("pe", lambda E, p=par, k=k, ps=ps: E.matmul(
                ps[:, 0:512], xfm[p].ap()[:, 128 * k:128 * (k + 1)],
                wqk_sb.ap()[:, 512 * k:512 * (k + 1)],
                start=(k == 0), stop=(k == 15)),
                reads=[xfk, "wqk_sb", psk], writes=[psk])
        for k in range(16):
            B.op("pe", lambda E, p=par, k=k, ps=ps: E.matmul(
                ps[:, 512:768], xfm[p].ap()[:, 128 * k:128 * (k + 1)],
                wv_sb.ap()[:, 256 * k:256 * (k + 1)],
                start=(k == 0), stop=(k == 15)),
                reads=[xfk, "wv_sb", psk], writes=[psk])
        B.op("act", lambda E, ps=ps, par=par: E.activation(
            qkvscr[par].ap()[:, :], ps[:, 0:768], AF.Copy),
            reads=[psk], writes=[f"qkvscr{par}", psk])
        qs, qb = qkvscr[par].ap(), qkvbf[par].ap()
        cs, sn = cost[par].ap(), sint[par].ap()
        ck, sk = f"cost{par}", f"sint{par}"
        scrk, qbk = f"qkvscr{par}", f"qkvbf{par}"

        def rope(engine, base, t0, t1, k0, k1, cs=cs, sn=sn, ck=ck, sk=sk):
            xe = qs[:, base + 0:base + 256:2]
            xo = qs[:, base + 1:base + 256:2]
            oe = qb[:, base + 0:base + 256:2]
            oo = qb[:, base + 1:base + 256:2]
            B.op(engine, lambda E, t0=t0, xe=xe, cs=cs: E.tensor_tensor(t0, xe, cs[:, :], ALU.mult),
                 reads=[scrk, ck], writes=[k0])
            B.op(engine, lambda E, t1=t1, xo=xo, sn=sn: E.tensor_tensor(t1, xo, sn[:, :], ALU.mult),
                 reads=[scrk, sk], writes=[k1])
            B.op(engine, lambda E, oe=oe, t0=t0, t1=t1: E.tensor_tensor(oe, t0, t1, ALU.subtract),
                 reads=[k0, k1], writes=[qbk])
            B.op(engine, lambda E, t0=t0, xe=xe, sn=sn: E.tensor_tensor(t0, xe, sn[:, :], ALU.mult),
                 reads=[scrk, sk], writes=[k0])
            B.op(engine, lambda E, t1=t1, xo=xo, cs=cs: E.tensor_tensor(t1, xo, cs[:, :], ALU.mult),
                 reads=[scrk, ck], writes=[k1])
            B.op(engine, lambda E, oo=oo, t0=t0, t1=t1: E.tensor_tensor(oo, t0, t1, ALU.add),
                 reads=[k0, k1], writes=[qbk])

        rbase = 512 * par
        rope("dve", 0, rsq.ap()[:, rbase:rbase + 128],
             rsq.ap()[:, rbase + 128:rbase + 256], f"rq0_{par}", f"rq1_{par}")
        rope("pool", 256, rsq.ap()[:, rbase + 256:rbase + 384],
             rsq.ap()[:, rbase + 384:rbase + 512], f"rk0_{par}", f"rk1_{par}")
        B.op("pool", lambda E, qb=qb, qs=qs: E.tensor_copy(qb[:, 512:768], qs[:, 512:768]),
             reads=[scrk], writes=[qbk])

    def p1_s2r(t):
        par = t % 2
        qb = qkvbf[par].ap()
        qbk = f"qkvbf{par}"
        pst2 = psC_bf if par == 0 else psD_bf
        pst2k = "ps_c" if par == 0 else "ps_d"
        srcs = [0, 256, 512, 128, 384, 640]
        for j, off in enumerate(srcs):
            B.op("pe", lambda E, off=off, j=j, qb=qb, pst2=pst2: E.transpose(
                pst2[:, 128 * j:128 * (j + 1)], qb[:, off:off + 128],
                ident.ap()[:, :]),
                reads=[qbk, "ident", pst2k], writes=[pst2k])
        dst = qkvfm.ap()[:, :].rearrange("p (s n) -> p s n", s=6)[
            :, :, 128 * t:128 * (t + 1)]
        eng = "act" if par == 0 else "dve"
        if eng == "act":
            B.op("act", lambda E, dst=dst, pst2=pst2: E.activation(
                dst, pst2[:, 0:768].rearrange("p (s n) -> p s n", s=6), AF.Copy),
                reads=[pst2k], writes=[f"qkvfm_{t}", pst2k])
        else:
            B.op("dve", lambda E, dst=dst, pst2=pst2: E.tensor_copy(
                dst, pst2[:, 0:768].rearrange("p (s n) -> p s n", s=6)),
                reads=[pst2k], writes=[f"qkvfm_{t}", pst2k])

    p1_s1(0)
    p1_s1(1)
    p1_s2t(0)
    for t in range(NCH):
        if t + 1 < NCH:
            p1_s2t(t + 1)
        p1_s2m(t)
        if t >= 1:
            p1_s2r(t - 1)
        if t + 2 < NCH:
            p1_s1(t + 2)
    p1_s2r(NCH - 1)
    qkv_all = [f"qkvfm_{t}" for t in range(NCH)]

    if "d_scr" in dbg_out:
        B.dma("act", "q_dbg", lambda E: E.dma_start(
            out=dbg_out["d_scr"], in_=qkvscr.ap()[:, :]), reads=["qkvscr"])
    if "d_rope" in dbg_out:
        B.dma("act", "q_dbg", lambda E: E.dma_start(
            out=dbg_out["d_rope"], in_=qkvbf.ap()[:, :]), reads=["qkvbf"])
    if "d_cs" in dbg_out:
        B.dma("act", "q_dbg", lambda E: E.dma_start(
            out=dbg_out["d_cs"][:, 0:128], in_=cost[1].ap()[:, :]), reads=["cost1"])
        B.dma("act", "q_dbg", lambda E: E.dma_start(
            out=dbg_out["d_cs"][:, 128:256], in_=sint[1].ap()[:, :]), reads=["sint1"])

    if "d_qkvfm" in dbg_out:
        for sl in range(6):
            B.dma("act", "q_dbg", lambda E, sl=sl: E.dma_start(
                out=dbg_out["d_qkvfm"][sl], in_=qkvfm.ap()[:, sl * S:(sl + 1) * S]),
                reads=qkv_all)

    if not phase_on("p2"):
        _finalize_build(nc, B)
        return _emit(nc, B)
    B.fence()
    B.dma("sync", "q_msk", lambda E: E.dma_start(
        out=masks_sb.ap()[:, :].rearrange("p (b q) -> p b q", b=4),
        in_=ap["masks"].rearrange("(b k) q -> k b q", b=4)),
        writes=["masks_sb"])
    for h in range(P2_HEADS):
        qslot, kslot, vslot = 3 * h, 3 * h + 1, 3 * h + 2
        qv = qkvfm.ap()[:, qslot * S:(qslot + 1) * S]
        kv = qkvfm.ap()[:, kslot * S:(kslot + 1) * S]
        vv = qkvfm.ap()[:, vslot * S:(vslot + 1) * S]
        for b in range(8):
            for i in range(8):
                cidx = 8 * b + i
                B.op("pe", lambda E, cidx=cidx, i=i, vv=vv: E.transpose(
                    psD_bf[:, 128 * i:128 * (i + 1)],
                    vv[:, 128 * cidx:128 * (cidx + 1)], ident.ap()[:, :]),
                    reads=qkv_all + ["ident", "ps_d"], writes=["ps_d"])
            B.op("dve", lambda E, b=b: E.tensor_copy(
                vtm.ap()[:, 1024 * b:1024 * (b + 1)], psD_bf[:, :]),
                reads=["ps_d"], writes=["vtm", "ps_d"])
        for qi in range(P2_QTILES):
            nk = 4 * (qi + 1)
            qrhs = qv[:, 512 * qi:512 * (qi + 1)]
            groups = []
            kc0 = 0
            while kc0 < nk:
                groups.append((kc0, min(3, nk - kc0)))
                kc0 += 3
            ng = len(groups)

            def emit_S(gi):
                kc0, gw = groups[gi]
                par = gi % 2
                ps = psA.ap() if par == 0 else psB.ap()
                psk = "ps_A" if par == 0 else "ps_B"
                for j in range(gw):
                    kc = kc0 + j
                    B.op("pe", lambda E, kc=kc, j=j, ps=ps, kv=kv, qrhs=qrhs:
                         E.matmul(
                        ps[:, 512 * j:512 * (j + 1)],
                        kv[:, 128 * kc:128 * (kc + 1)], qrhs,
                        start=True, stop=True),
                        reads=qkv_all + [psk], writes=[psk])

            def emit_exp_mask(gi):
                kc0, gw = groups[gi]
                par = gi % 2
                ps = psA.ap() if par == 0 else psB.ap()
                psk = "ps_A" if par == 0 else "ps_B"
                pbk = f"pb{gi % 3}"
                pba = pb[gi % 3].ap()
                B.op("act", lambda E, gw=gw, ps=ps, pba=pba: E.activation(
                    pba[:, 0:512 * gw], ps[:, 0:512 * gw], AF.Exp,
                    bias=c_nshift, scale=c_scale),
                    reads=[psk, "consts"], writes=[pbk, psk])
                for j in range(gw):
                    kc = kc0 + j
                    if kc >= nk - 4:
                        bmask = kc - (nk - 4)
                        B.op("dve", lambda E, j=j, bm=bmask, pba=pba: E.tensor_tensor(
                            pba[:, 512 * j:512 * (j + 1)],
                            pba[:, 512 * j:512 * (j + 1)],
                            masks_sb.ap()[:, 512 * bm:512 * (bm + 1)], ALU.mult),
                            reads=[pbk, "masks_sb"], writes=[pbk])

            def emit_den_pv(gi):
                kc0, gw = groups[gi]
                pbk = f"pb{gi % 3}"
                pba = pb[gi % 3].ap()
                pk = f"padd{gi % 2}"
                pa = padd[gi % 2].ap()
                if gw == 1:
                    dsrc, dsk = pba[:, 0:512], pbk
                else:
                    B.op("dve", lambda E, pba=pba, pa=pa: E.tensor_tensor(
                        pa[:, :], pba[:, 0:512], pba[:, 512:1024], ALU.add),
                        reads=[pbk], writes=[pk])
                    if gw == 3:
                        B.op("dve", lambda E, pba=pba, pa=pa: E.tensor_tensor(
                            pa[:, :], pa[:, :], pba[:, 1024:1536], ALU.add),
                            reads=[pk, pbk], writes=[pk])
                    dsrc, dsk = pa[:, :], pk
                B.op("pe", lambda E, dsrc=dsrc, st=(gi == 0), sp=(gi == ng - 1):
                     E.matmul(
                    psD.ap()[0:1, 0:512], onesb.ap()[:, :], dsrc,
                    start=st, stop=sp),
                    reads=[dsk, "onesb", "ps_d"], writes=["ps_d"])
                for j in range(gw):
                    kc = kc0 + j
                    B.op("pe", lambda E, kc=kc, j=j, pba=pba,
                         st=(gi == 0 and j == 0),
                         sp=(gi == ng - 1 and j == gw - 1): E.matmul(
                        psC.ap()[:, 0:512], vtm.ap()[:, 128 * kc:128 * (kc + 1)],
                        pba[:, 512 * j:512 * (j + 1)], start=st, stop=sp),
                        reads=["vtm", pbk, "ps_ctx"], writes=["ps_ctx"])

            emit_S(0)
            for gi in range(ng):
                if gi + 1 < ng:
                    emit_S(gi + 1)
                emit_exp_mask(gi)
                emit_den_pv(gi)
            B.op("dve", lambda E: E.reciprocal(
                srow.ap()[0:1, 0:512], psD.ap()[0:1, 0:512]),
                reads=["ps_d"], writes=["srow", "ps_d"])
            B.op("pe", lambda E: E.matmul(
                psD.ap()[:, 0:512], onesf.ap()[:, :], srow.ap()[0:1, 0:512],
                start=True, stop=True),
                reads=["srow", "onesf", "ps_d"], writes=["ps_d"])
            B.op("act", lambda E: E.activation(
                bcsb.ap()[:, 0:512], psD.ap()[:, 0:512], AF.Copy),
                reads=["ps_d"], writes=["bcsb", "ps_d"])
            B.op("dve", lambda E: E.tensor_tensor(
                cnsb.ap()[:, :], psC.ap()[:, 0:512], bcsb.ap()[:, 0:512], ALU.mult),
                reads=["ps_ctx", "bcsb"], writes=["cnsb", "ps_ctx"])
            for j in range(4):
                B.op("pe", lambda E, j=j: E.transpose(
                    psD_bf[:, 128 * j:128 * (j + 1)],
                    cnsb.ap()[:, 128 * j:128 * (j + 1)], ident.ap()[:, :]),
                    reads=["cnsb", "ident", "ps_d"], writes=["ps_d"])
            B.op("dve", lambda E, h=h, qi=qi: E.tensor_copy(
                ctxtm[h].ap()[:, 512 * qi:512 * (qi + 1)], psD_bf[:, 0:512]),
                reads=["ps_d"], writes=[f"ctxtm{h}_{qi}", "ps_d"])
        ctxkeys = [f"ctxtm{h}_{qi}" for qi in range(P2_QTILES)]
        if P2_HEADS < 2 or P2_QTILES < 16:
            B.dma("act", f"q_cc{h}", lambda E, h=h: E.dma_start(
                out=cc_in[h].rearrange("(c p) d -> p c d", p=128),
                in_=ctxtm[h].ap()[:, :].rearrange("p (c d) -> p c d", d=128)),
                reads=ctxkeys, writes=[f"cc_in{h}"])
            B.dma_group("act", f"q_ag{h}", [
                (lambda E, h=h, r=r: E.dma_start(
                    out=cc_out[h][S * r:S * (r + 1), :], in_=cc_in[h][:, :]))
                for r in range(NCORES)],
                reads=[f"cc_in{h}"], writes=[f"cc_out{h}"])
            continue
        B.dma("act", f"q_cc{h}", lambda E, h=h: E.dma_start(
            out=cc_in[h].rearrange("(c p) d -> p c d", p=128),
            in_=ctxtm[h].ap()[:, :].rearrange("p (c d) -> p c d", d=128)),
            reads=ctxkeys, writes=[f"cc_in{h}"])
        B.collective(lambda E, h=h: E.collective_compute(
            "AllGather", ALU.bypass, replica_groups=[list(range(NCORES))],
            ins=[cc_in[h][:, :].opt()], outs=[cc_out[h][:, :].opt()]),
            f"q_ag{h}", reads=[f"cc_in{h}"], writes=[f"cc_out{h}"])

    if "d_ctxtm" in dbg_out:
        for h in range(2):
            B.dma("act", "q_dbg", lambda E, h=h: E.dma_start(
                out=dbg_out["d_ctxtm"][h], in_=ctxtm[h].ap()[:, :]),
                reads=[f"ctxtm{h}_{qi}" for qi in range(16)])
    if "d_vtm" in dbg_out:
        B.dma("act", "q_dbg", lambda E: E.dma_start(
            out=dbg_out["d_vtm"], in_=vtm.ap()[:, :]), reads=["vtm"])
    if "d_pb" in dbg_out:
        for i in range(3):
            B.dma("act", "q_dbg", lambda E, i=i: E.dma_start(
                out=dbg_out["d_pb"][:, 1536 * i:1536 * (i + 1)],
                in_=pb[i].ap()[:, :]), reads=[f"pb{i}"])
    if "d_den" in dbg_out:
        B.dma("act", "q_dbg", lambda E: E.dma_start(
            out=dbg_out["d_den"], in_=srow.ap()[:, 0:512]), reads=["srow"])
    if "d_cn" in dbg_out:
        B.dma("act", "q_dbg", lambda E: E.dma_start(
            out=dbg_out["d_cn"], in_=cnsb.ap()[:, :]), reads=["cnsb"])
    if "d_bc" in dbg_out:
        B.dma("act", "q_dbg", lambda E: E.dma_start(
            out=dbg_out["d_bc"], in_=bcsb.ap()[:, :]), reads=["bcsb"])

    if not phase_on("gather"):
        _finalize_build(nc, B)
        return _emit(nc, B)
    B.fence()
    B.dma("pool", "q_wo", lambda E: E.dma_start(
        out=wo_sb.ap()[:, :].rearrange("p (k f) -> p k f", k=16), in_=ap["wo_t"]),
        writes=["wo_sb"])
    for h in range(2):
        B.dma("pool", f"q_gth{h}", lambda E, h=h: E.indirect_dma_start(
            gth[h].ap()[:, :], None, cc_out[h],
            IndirectOffsetOnAxis(idxsb.ap()[:, :], 0)),
            reads=[f"cc_out{h}", "idxsb"], writes=[f"gth{h}"])
        B.dma_group("pool", f"q_cto{h}", [
            (lambda E, h=h, r=r: E.dma_start(
                out=ctx_own[r, h].rearrange("(i t) d -> i (t d)", i=16),
                in_=gth[h].ap()[16 * r:16 * (r + 1), :]))
            for r in range(8)], reads=[f"gth{h}"], writes=[f"ctx_own{h}"])
    if "d_ctxown" in dbg_out:
        B.dma("act", "q_dbg", lambda E: E.dma_start(
            out=dbg_out["d_ctxown"].rearrange("r h t d -> (r h) (t d)"),
            in_=ctx_own.rearrange("r h t d -> (r h) (t d)")),
            reads=["ctx_own0", "ctx_own1"])

    if not phase_on("p3"):
        _finalize_build(nc, B)
        return _emit(nc, B)
    def p3_r(t):
        par = t % 2
        xk = f"xl2t{par}"
        for half in range(2):
            for i in range(8):
                k = 8 * half + i
                B.op("pe", lambda E, k=k, i=i, par=par: E.transpose(
                    psD_bf[:, 128 * i:128 * (i + 1)],
                    xl2t[par].ap()[:, 128 * k:128 * (k + 1)], ident.ap()[:, :]),
                    reads=[xk, "ident", "ps_d"], writes=["ps_d"])
            dst = xln2fm.ap()[:, :].rearrange("p (k n) -> p k n", k=16)[
                :, 8 * half:8 * (half + 1), 128 * t:128 * (t + 1)]
            B.op("dve", lambda E, dst=dst: E.tensor_copy(
                dst, psD_bf[:, :].rearrange("p (k n) -> p k n", k=8)),
                reads=["ps_d"], writes=[f"xln2fm_{t}", "ps_d"])

    for t in range(OCH):
        par = t % 2
        ctk = f"ctxT{par}"
        B.dma_group("sync", f"q_ctxT{par}", [
            (lambda E, r=g // 2, hh=g % 2, t=t, g=g, par=par:
             E.dma_start(out=ctxT[par].ap()[:, 128 * g:128 * (g + 1)],
                         in_=ctx_own[r, hh, 128 * t:128 * (t + 1), :],
                         transpose=True))
            for g in range(16)],
            reads=["ctx_own0", "ctx_own1"], writes=[ctk])
        B.dma("sync", f"q_xo{par}", lambda E, t=t, par=par: E.dma_start(
            out=xob[par].ap()[:, :], in_=ap["xo"][128 * t:128 * (t + 1), :]),
            writes=[f"xob{par}"])
        outs = [psA.ap()[:, 0:512], psA.ap()[:, 512:1024],
                psA.ap()[:, 1024:1536], psB.ap()[:, 0:512]]
        pkeys = ["ps_A", "ps_A", "ps_A", "ps_B"]
        for n in range(4):
            for k in range(16):
                B.op("pe", lambda E, n=n, k=k, par=par: E.matmul(
                    outs[n], ctxT[par].ap()[:, 128 * k:128 * (k + 1)],
                    wo_sb.ap()[:, 2048 * k + 512 * n:2048 * k + 512 * (n + 1)],
                    start=(k == 0), stop=(k == 15)),
                    reads=[ctk, "wo_sb", pkeys[n]], writes=[pkeys[n]])
        St = STATS[par]
        sqa = sq3[par].ap()
        sqk = f"sq3_{par}"
        B.op("dve", lambda E, sqa=sqa, St=St: E.tensor_scalar(
            sqa[:, 0:1536], psA.ap()[:, :], 1.0, None, ALU.mult, ALU.add,
            accum_out=St.sum),
            reads=["ps_A"], writes=[sqk, St.k["sum"]])
        B.op("dve", lambda E, sqa=sqa, St=St: E.tensor_scalar(
            sqa[:, 1536:2048], psB.ap()[:, 0:512], 1.0, None, ALU.mult, ALU.add,
            accum_out=St.t0),
            reads=["ps_B"], writes=[sqk, St.k["t0"]])
        B.op("act", lambda E, sqa=sqa, St=St: E.activation(
            sqa[:, 0:1536], psA.ap()[:, :], AF.Square, accum_out=St.sq),
            reads=["ps_A", sqk], writes=[sqk, St.k["sq"]])
        B.op("act", lambda E, sqa=sqa, St=St: E.activation(
            sqa[:, 1536:2048], psB.ap()[:, 0:512], AF.Square, accum_out=St.t1),
            reads=["ps_B", sqk], writes=[sqk, St.k["t1"]])
        B.op("dve", lambda E, St=St: E.tensor_tensor(St.sum, St.sum, St.t0, ALU.add),
             reads=[St.k["sum"], St.k["t0"]], writes=[St.k["sum"]])
        B.op("dve", lambda E, St=St: E.tensor_tensor(St.sq, St.sq, St.t1, ALU.add),
             reads=[St.k["sq"], St.k["t1"]], writes=[St.k["sq"]])
        ln_finalize(St)
        B.op("dve", lambda E, par=par, St=St: E.tensor_scalar(
            x1b[par].ap()[:, 0:1536], psA.ap()[:, :], St.rstd, St.nmr,
            ALU.mult, ALU.add),
            reads=["ps_A", St.k["rstd"], St.k["nmr"]], writes=[f"x1b{par}", "ps_A"])
        B.op("dve", lambda E, par=par, St=St: E.tensor_scalar(
            x1b[par].ap()[:, 1536:2048], psB.ap()[:, 0:512], St.rstd, St.nmr,
            ALU.mult, ALU.add),
            reads=["ps_B", St.k["rstd"], St.k["nmr"]], writes=[f"x1b{par}", "ps_B"])
        B.op("pool", lambda E, par=par: E.tensor_tensor(
            x1b[par].ap()[:, :], x1b[par].ap()[:, :], xob[par].ap()[:, :], ALU.add),
            reads=[f"x1b{par}", f"xob{par}"], writes=[f"x1b{par}"])
        B.dma("act", f"q_x1{par}", lambda E, t=t, par=par: E.dma_start(
            out=x1_dram[128 * t:128 * (t + 1), :], in_=x1b[par].ap()[:, :]),
            reads=[f"x1b{par}"], writes=[f"x1d_{t}"])
        B.op("dve", lambda E, par=par, sqa=sqa, St=St: E.tensor_scalar(
            sqa[:, :], x1b[par].ap()[:, :], 1.0, None, ALU.mult, ALU.add,
            accum_out=St.sum),
            reads=[f"x1b{par}", sqk], writes=[sqk, St.k["sum"]])
        B.op("act", lambda E, par=par, sqa=sqa, St=St: E.activation(
            sqa[:, :], x1b[par].ap()[:, :], AF.Square, accum_out=St.sq),
            reads=[f"x1b{par}", sqk], writes=[sqk, St.k["sq"]])
        ln_finalize(St)
        B.op("act", lambda E, par=par, St=St: E.activation(
            xl2t[par].ap()[:, :], x1b[par].ap()[:, :], AF.Identity, bias=St.nmr,
            scale=St.rstd),
            reads=[f"x1b{par}", St.k["rstd"], St.k["nmr"]], writes=[f"xl2t{par}"])
        if t >= 1:
            p3_r(t - 1)
    p3_r(OCH - 1)
    xln2_all = [f"xln2fm_{t}" for t in range(OCH)]
    if "d_x1" in dbg_out:
        B.dma("act", "q_dbg", lambda E: E.dma_start(
            out=dbg_out["d_x1"], in_=x1_dram[:, :]),
            reads=[f"x1d_{t}" for t in range(OCH)])
    if "d_xln2fm" in dbg_out:
        B.dma("act", "q_dbg", lambda E: E.dma_start(
            out=dbg_out["d_xln2fm"], in_=xln2fm.ap()[:, :]), reads=xln2_all)

    if not phase_on("p4"):
        _finalize_build(nc, B)
        return _emit(nc, B)
    B.dma("pool", "q_gate", lambda E: E.dma_start(
        out=gsb.ap()[:, :].rearrange("p (k f) -> p k f", k=16), in_=ap["gate_t"]),
        writes=["gate_sb"])
    x2v = xln2fm.ap()[:, :].rearrange("p (k n) -> p k n", k=16)
    for t in range(OCH):
        for k in range(16):
            B.op("pe", lambda E, t=t, k=k: E.matmul(
                psC.ap()[:, 0:NEXP], x2v[:, k, 128 * t:128 * (t + 1)],
                gsb.ap()[:, NEXP * k:NEXP * (k + 1)],
                start=(k == 0), stop=(k == 15)),
                reads=xln2_all + ["gate_sb", "ps_ctx"], writes=["ps_ctx"])
        St = STATS[t % 2]
        B.op("dve", lambda E: E.tensor_copy(lg_sb.ap()[:, 0:8], psC.ap()[:, 0:8]),
             reads=["ps_ctx"], writes=["lg", "ps_ctx"])
        B.op("dve", lambda E, St=St: E.reduce_max(St.t0, lg_sb.ap()[:, 0:8], axis=AX),
             reads=["lg"], writes=[St.k["t0"]])
        B.op("dve", lambda E, St=St: E.tensor_scalar(
            lg_sb.ap()[:, 8:16], lg_sb.ap()[:, 0:8], St.t0, None, ALU.is_ge),
            reads=["lg", St.k["t0"]], writes=["ge"])
        B.op("dve", lambda E: E.scalar_tensor_tensor(
            lg_sb.ap()[:, 16:24], lg_sb.ap()[:, 8:16], c_neg, lg_sb.ap()[:, 0:8],
            ALU.mult, ALU.add),
            reads=["ge", "lg", "consts"], writes=["l2"])
        B.op("dve", lambda E, St=St: E.reduce_max(St.t1, lg_sb.ap()[:, 16:24], axis=AX),
             reads=["l2"], writes=[St.k["t1"]])
        B.op("dve", lambda E, t=t, St=St: E.tensor_tensor(
            top2c.ap()[:, t:t + 1], St.t0, St.t1, ALU.add),
            reads=[St.k["t0"], St.k["t1"]], writes=["top2c"])
    if "d_top2" in dbg_out:
        B.dma("act", "q_dbg", lambda E: E.dma_start(
            out=dbg_out["d_top2"], in_=top2c.ap()[:, :]), reads=["top2c"])
    B.op("pe", lambda E: E.transpose(
        psD.ap()[0:8, 0:128], top2c.ap()[:, :], identf.ap()[:, :]),
        reads=["top2c", "identf", "ps_d"], writes=["ps_d"])
    B.op("dve", lambda E: E.tensor_copy(t2tmp.ap()[:, :], psD.ap()[0:8, 0:128]),
         reads=["ps_d"], writes=["t2tmp", "ps_d"])
    for par in range(2):
        B.wait("sync", B.dma_sem(f"q_ctxT{par}"),
               16 * B.dma_count.get(f"q_ctxT{par}", 0))
    B.dma("sync", "q_s2s", lambda E: E.dma_start(
        out=sr2.ap()[0:1, :], in_=t2tmp.ap()[:, :]),
        reads=["t2tmp"], writes=["sr2"])
    for n in range(2):
        B.op("pe", lambda E, n=n: E.matmul(
            psC.ap()[:, 0:512], onesf.ap()[:, :],
            sr2.ap()[0:1, 512 * n:512 * (n + 1)], start=True, stop=True),
            reads=["sr2", "onesf", "ps_ctx"], writes=["ps_ctx"])
        B.op("act", lambda E, n=n: E.activation(
            bc2.ap()[:, 512 * n:512 * (n + 1)], psC.ap()[:, 0:512], AF.Copy),
            reads=["ps_ctx"], writes=["bc2", "ps_ctx"])

    if not phase_on("p5"):
        _finalize_build(nc, B)
        return _emit(nc, B)
    slots = [("sl0", psA.ap()[:, 0:512]), ("sl1", psA.ap()[:, 512:1024]),
             ("sl2", psA.ap()[:, 1024:1536]), ("sl3", psB.ap()[:, 0:512]),
             ("sl4", psB.ap()[:, 512:1024]), ("sl5", psB.ap()[:, 1024:1536])]
    B.fence()
    for eg in range(32):
        wpar = eg % 2
        wk = f"wup{wpar}"
        B.dma("pool", f"q_up{wpar}", lambda E, eg=eg, wpar=wpar: E.dma_start(
            out=wup[wpar].ap()[:, :], in_=ap["up_t"][eg].rearrange(
                "p k i f -> p (k i f)")),
            writes=[wk])
        for i in range(2):
            e = 2 * eg + i
            s0k, s0 = slots[(2 * e) % 6]
            s1k, s1 = slots[(2 * e + 1) % 6]
            for k in range(16):
                for n in range(2):
                    sl, slk = (s0, s0k) if n == 0 else (s1, s1k)
                    B.op("pe", lambda E, k=k, i=i, n=n, sl=sl, wpar=wpar: E.matmul(
                        sl, wup[wpar].ap()[:, (2 * k + i) * 128:(2 * k + i + 1) * 128],
                        xln2fm.ap()[:, 1024 * k + 512 * n:1024 * k + 512 * (n + 1)],
                        start=(k == 0), stop=(k == 15)),
                        reads=[wk] + xln2_all + [slk], writes=[slk])
            for n in range(2):
                sl, slk = (s0, s0k) if n == 0 else (s1, s1k)
                B.op("dve", lambda E, n=n, sl=sl: E.tensor_scalar(
                    rsc.ap()[:, 512 * n:512 * (n + 1)], sl, 0.0, None, ALU.max),
                    reads=[slk], writes=["rsc", slk])
            B.op("act", lambda E, e=e: E.activation(
                H_sb.ap()[:, 1024 * e:1024 * (e + 1)], rsc.ap()[:, :], AF.Square),
                reads=["rsc"], writes=[f"H_{e}"])
    H_all = [f"H_{e}" for e in range(64)]
    if "d_h" in dbg_out:
        B.dma("act", "q_dbg", lambda E: E.dma_start(
            out=dbg_out["d_h"], in_=H_sb.ap()[:, :]), reads=H_all)

    for k in range(16):
        B.op("dve", lambda E, k=k: E.tensor_tensor(
            xln2fm.ap()[:, 1024 * k:1024 * (k + 1)],
            xln2fm.ap()[:, 1024 * k:1024 * (k + 1)], bc2.ap()[:, :], ALU.mult),
            reads=xln2_all + ["bc2"],
            writes=[f"x2s_{k}"] + (xln2_all if k == 0 else []))
    x2s_all = [f"x2s_{k}" for k in range(16)]

    if not phase_on("p6"):
        _finalize_build(nc, B)
        return _emit(nc, B)
    if "d_bc2" in dbg_out:
        B.dma("act", "q_dbg", lambda E: E.dma_start(
            out=dbg_out["d_bc2"], in_=bc2.ap()[:, :]), reads=["bc2"])
    if "d_x2s" in dbg_out:
        B.dma("act", "q_dbg", lambda E: E.dma_start(
            out=dbg_out["d_x2s"], in_=xln2fm.ap()[:, :]), reads=x2s_all)

    for dg in range(8):
        dpair = []
        for i in range(2):
            d = 2 * dg + i
            s0k, s0 = slots[(2 * d) % 6]
            s1k, s1 = slots[(2 * d + 1) % 6]
            dpair.append(((s0k, s0), (s1k, s1)))
        for sub in range(8):
            wpar = sub % 2
            wk = f"wdn{wpar}"
            B.dma("pool", f"q_dn{wpar}", lambda E, dg=dg, sub=sub, wpar=wpar:
                  E.dma_start(out=wdn[wpar].ap()[:, :],
                              in_=ap["down_t"][dg][:, 8 * sub:8 * (sub + 1)]
                              .rearrange("p k i f -> p (k i f)")),
                  writes=[wk])
            for k in range(8 * sub, 8 * sub + 8):
                koff = k % 8
                for i in range(2):
                    (s0k, s0), (s1k, s1) = dpair[i]
                    for n in range(2):
                        sl, slk = (s0, s0k) if n == 0 else (s1, s1k)
                        B.op("pe", lambda E, koff=koff, i=i, n=n, sl=sl,
                             wpar=wpar, k=k: E.matmul(
                            sl, wdn[wpar].ap()[:, (2 * koff + i) * 128:(2 * koff + i + 1) * 128],
                            H_sb.ap()[:, 1024 * k + 512 * n:1024 * k + 512 * (n + 1)],
                            start=(k == 0), stop=False),
                            reads=[wk] + H_all + [slk], writes=[slk])
        for sub in range(2):
            B.dma("pool", f"q_dn{sub}", lambda E, dg=dg, sub=sub: E.dma_start(
                out=wdn[sub].ap()[:, 0:2048],
                in_=ap["exp_t"][dg][:, 8 * sub:8 * (sub + 1)]
                .rearrange("p k i f -> p (k i f)")),
                writes=[f"wex{sub}", f"wdn{sub}"])
            for k in range(8 * sub, 8 * sub + 8):
                koff = k % 8
                for i in range(2):
                    (s0k, s0), (s1k, s1) = dpair[i]
                    for n in range(2):
                        sl, slk = (s0, s0k) if n == 0 else (s1, s1k)
                        B.op("pe", lambda E, koff=koff, i=i, n=n, sl=sl,
                             sub=sub, k=k: E.matmul(
                            sl, wdn[sub].ap()[:, (2 * koff + i) * 128:(2 * koff + i + 1) * 128],
                            xln2fm.ap()[:, 1024 * k + 512 * n:1024 * k + 512 * (n + 1)],
                            start=False, stop=(k == 15)),
                            reads=[f"wdn{sub}"] + x2s_all + [slk], writes=[slk])
        for i in range(2):
            d = 2 * dg + i
            ypar = i
            (s0k, s0), (s1k, s1) = dpair[i]
            B.op("dve", lambda E, s0=s0, ypar=ypar: E.tensor_copy(
                ysc[ypar].ap()[:, 0:512], s0),
                reads=[s0k], writes=[f"ysc{ypar}", s0k, "rsc"])
            B.op("dve", lambda E, s1=s1, ypar=ypar: E.tensor_copy(
                ysc[ypar].ap()[:, 512:1024], s1), reads=[s1k], writes=[f"ysc{ypar}", s1k])
            for half in range(2):
                for j in range(4):
                    tc = 4 * half + j
                    B.op("pe", lambda E, tc=tc, j=j, ypar=ypar: E.transpose(
                        psC.ap()[:, 128 * j:128 * (j + 1)],
                        ysc[ypar].ap()[:, 128 * tc:128 * (tc + 1)],
                        identf.ap()[:, :]),
                        reads=[f"ysc{ypar}", "identf", "ps_ctx"], writes=["ps_ctx"])
                B.op("dve", lambda E, half=half: E.tensor_copy(
                    yst.ap()[:, 512 * half:512 * (half + 1)], psC.ap()[:, :]),
                    reads=["ps_ctx"], writes=["yst", "ps_ctx"])
            B.dma("act", "q_y", lambda E, d=d: E.dma_start(
                out=y_dram[d].rearrange("c p f -> p c f"),
                in_=yst.ap()[:, :].rearrange("p (c f) -> p c f", c=8)),
                reads=["yst"], writes=[f"yd_{d}"])
    if "d_y" in dbg_out:
        B.dma("act", "q_dbg", lambda E: E.dma_start(
            out=dbg_out["d_y"].rearrange("d c p f -> (d c) (p f)"),
            in_=y_dram.rearrange("d c p f -> (d c) (p f)")),
            reads=[f"yd_{d}" for d in range(16)])

    if not phase_on("p7"):
        _finalize_build(nc, B)
        return _emit(nc, B)
    B.fence()
    B.dma_group("sync", "q_rows", [
        lambda E: E.dma_start(out=grow.ap()[:, :], in_=ap["lng"]),
        lambda E: E.dma_start(out=browt.ap()[:, :], in_=ap["lnb"]),
    ], writes=["grow", "brw"])
    for n in range(4):
        B.op("pe", lambda E, n=n: E.matmul(
            psD.ap()[:, 0:512], onesf.ap()[:, :],
            grow.ap()[0:1, 512 * n:512 * (n + 1)], start=True, stop=True),
            reads=["grow", "onesf", "ps_d"], writes=["ps_d"])
        B.op("act", lambda E, n=n: E.activation(
            gamb.ap()[:, 512 * n:512 * (n + 1)], psD.ap()[:, 0:512], AF.Copy),
            reads=["ps_d"], writes=["gamb", "ps_d"])
    for n in range(4):
        B.op("pe", lambda E, n=n: E.matmul(
            psD.ap()[:, 0:512], onesf.ap()[:, :],
            browt.ap()[0:1, 512 * n:512 * (n + 1)], start=True, stop=True),
            reads=["brw", "onesf", "ps_d"], writes=["ps_d"])
        B.op("act", lambda E, n=n: E.activation(
            betb.ap()[:, 512 * n:512 * (n + 1)], psD.ap()[:, 0:512], AF.Copy),
            reads=["ps_d"], writes=["betb", "ps_d"])
    def p7_load(t):
        par = t % 3
        ybk, x1k = f"yb{par}", f"x1r{par}"
        B.dma("sync", f"q_yb{par}", lambda E, t=t, par=par: E.dma_start(
            out=yb[par].ap()[:, :].rearrange("p (d f) -> p d f", d=16),
            in_=y_dram[:, t].rearrange("d p f -> p d f")),
            reads=[f"yd_{d}" for d in range(16)], writes=[ybk])
        B.dma("sync", f"q_x1r{par}", lambda E, t=t, par=par: E.dma_start(
            out=x1r[par].ap()[:, :], in_=x1_dram[128 * t:128 * (t + 1), :]),
            reads=[f"x1d_{t}"], writes=[x1k])

    def p7_a(t):
        par = t % 3
        spar = t % 2
        ybk = f"yb{par}"
        ya = yb[par].ap()
        St = STATS[spar]
        sqa = sq3[spar].ap()
        sqk = f"sq3_{spar}"
        B.op("dve", lambda E, ya=ya, sqa=sqa, St=St: E.tensor_scalar(
            sqa[:, :], ya[:, :], 1.0, None, ALU.mult, ALU.add, accum_out=St.sum),
            reads=[ybk, sqk], writes=[sqk, St.k["sum"]])
        B.op("act", lambda E, ya=ya, sqa=sqa, St=St: E.activation(
            sqa[:, :], ya[:, :], AF.Square, accum_out=St.sq),
            reads=[ybk, sqk], writes=[sqk, St.k["sq"]])
        ln_finalize(St)
        B.op("dve", lambda E, ya=ya, St=St: E.tensor_scalar(
            ya[:, :], ya[:, :], St.rstd, St.nmr, ALU.mult, ALU.add),
            reads=[ybk, St.k["rstd"], St.k["nmr"]], writes=[ybk])
        B.op("pool", lambda E, ya=ya: E.tensor_tensor(
            ya[:, :], ya[:, :], gamb.ap()[:, :], ALU.mult),
            reads=[ybk, "gamb"], writes=[ybk])
        B.op("pool", lambda E, ya=ya: E.tensor_tensor(
            ya[:, :], ya[:, :], betb.ap()[:, :], ALU.add),
            reads=[ybk, "betb"], writes=[ybk])

    def p7_b(t):
        par = t % 3
        spar = t % 2
        ybk, x1k = f"yb{par}", f"x1r{par}"
        ya = yb[par].ap()
        St = STATS[spar]
        sqa = sq3[spar].ap()
        sqk = f"sq3_{spar}"
        B.op("dve", lambda E, ya=ya, sqa=sqa, St=St: E.tensor_scalar(
            sqa[:, :], ya[:, :], 1.0, None, ALU.mult, ALU.add, accum_out=St.sum),
            reads=[ybk, sqk], writes=[sqk, St.k["sum"]])
        B.op("act", lambda E, ya=ya, sqa=sqa, St=St: E.activation(
            sqa[:, :], ya[:, :], AF.Square, accum_out=St.sq),
            reads=[ybk, sqk], writes=[sqk, St.k["sq"]])
        ln_finalize(St)
        obk = f"ob{par}"
        B.op("dve", lambda E, ya=ya, St=St, par=par: E.tensor_scalar(
            ob[par].ap()[:, :], ya[:, :], St.rstd, St.nmr, ALU.mult, ALU.add),
            reads=[ybk, St.k["rstd"], St.k["nmr"]], writes=[obk])
        B.op("pool", lambda E, par=par: E.tensor_tensor(
            ob[par].ap()[:, :], ob[par].ap()[:, :], x1r[par].ap()[:, :], ALU.add),
            reads=[obk, x1k], writes=[obk])
        B.dma("act", f"q_out{par}", lambda E, t=t, par=par: E.dma_start(
            out=out[128 * t:128 * (t + 1), :], in_=ob[par].ap()[:, :]),
            reads=[obk], writes=[f"out_{t}"])

    p7_load(0)
    p7_load(1)
    p7_load(2)
    p7_a(0)
    for t in range(OCH):
        if t + 1 < OCH:
            p7_a(t + 1)
        p7_b(t)
        if t + 3 < OCH:
            p7_load(t + 3)

    _finalize_build(nc, B)
    return _emit(nc, B)


def _finalize_build(nc, B):
    for qo in ("q_out0", "q_out1", "q_out2"):
        if qo in B.dma_count:
            B.wait("sync", B.dma_sem(qo), 16 * B.dma_count[qo])
    else:
        pass
    for qd in ("q_dbg",):
        if qd in B.dma_count:
            B.wait("sync", qd, 16 * B.dma_count[qd])
    B.fence()
    for s in list(B.nofence_sems):
        if B.dma_count.get(s, 0) > 0:
            B.wait("sync", s, B.dma_count[s])

def _emit(nc, B):
    import contextlib
    with contextlib.ExitStack() as stack:
        for s in B.sem_names:
            B.sems[s] = stack.enter_context(nc.semaphore(s))
        block = stack.enter_context(nc.Block())
        B.replay(block)
    return nc


_CACHED = {}


def kernel(L0_attn_x, L0_attn_Wq, L0_attn_Wk, L0_attn_Wv, L0_attn_Wo,
           L0_ffn_gate_w, L0_ffn_up_w, L0_ffn_down_w, L0_ffn_expert_weight,
           L0_ffn_ln_gamma, L0_ffn_ln_beta, _debug=(), _nc=None, _upto="p7"):
    args = [np.asarray(a, dtype=np.float32) for a in (
        L0_attn_x, L0_attn_Wq, L0_attn_Wk, L0_attn_Wv, L0_attn_Wo,
        L0_ffn_gate_w, L0_ffn_up_w, L0_ffn_down_w, L0_ffn_expert_weight,
        L0_ffn_ln_gamma, L0_ffn_ln_beta)]
    shared = host_shared(*args)
    in_maps = [host_inputs(c, *args, shared) for c in range(NCORES)]
    key = (tuple(sorted(_debug)), _upto)
    if _nc is not None:
        nc = _nc
    else:
        if key not in _CACHED:
            _CACHED[key] = build_program(debug=_debug, upto=_upto)
        nc = _CACHED[key]
    res = run_bass_kernel_spmd(nc, in_maps, core_ids=list(range(NCORES)))
    outs = np.concatenate([res.results[c]["out"] for c in range(NCORES)], axis=0)
    if _debug:
        return outs.reshape(1, S, D), res
    return outs.reshape(1, S, D)
